# version 21
# baseline (speedup 1.0000x reference)
import math
import numpy as np
import concourse.bass as bass
import concourse.mybir as mybir
import concourse.bass_utils as _bu
from concourse.bass_utils import run_bass_kernel_spmd

if not getattr(_bu, "_ldw_patched", False):
    _orig_run_command = _bu.run_command

    def _run_command_ldw(cmd, *a, **kw):
        try:
            cmd = ["--enable-ldw-opt=true" if c == "--enable-ldw-opt=false" else c
                   for c in cmd]
        except TypeError:
            pass
        return _orig_run_command(cmd, *a, **kw)

    _bu.run_command = _run_command_ldw
    _bu._ldw_patched = True


F32 = mybir.dt.float32
BF16 = mybir.dt.bfloat16
AF = mybir.ActivationFunctionType
ALU = mybir.AluOpType
AX = mybir.AxisListType

D = 1024
L = 2048
NH = 16
NP = 8
NC = 16
NE = 16
NDT = 8
EPS = 1e-5
GAMMAS = [1.0 - 2.0 ** (-5 - math.floor(8 * i / NH)) for i in range(NH)]

ARENA_W = 53120
ENGINES = ("sync", "gpsimd", "tensor", "vector", "scalar")


def A(off, n):
    return ("A", off, n)


def P(off, n):
    return ("P", off, n)


def _keys(item):
    if isinstance(item, str):
        return (item,)
    kind, off, n = item
    if kind == "A":
        return tuple(f"a{p}" for p in range(off // 128, (off + n - 1) // 128 + 1))
    return tuple(f"p{b}" for b in range(off // 512, (off + n - 1) // 512 + 1))


class Builder:

    NDMASEM = 4

    def __init__(self):
        self.ops = []
        self.res = {}
        self.eng_of = []
        self.sem_of = []
        self.psum_reads = {}
        self.rr_edges = True
        self.dma_idx = {"sync": 0, "gpsimd": 0}
        self.last_dma_on_sem = {}
        self.hwm = {e: {} for e in ENGINES}

    def op(self, eng, emit, reads=(), writes=(), dma=False, wait_all_of=None):
        opid = len(self.ops)
        if dma:
            qi = self.dma_idx[eng]
            self.dma_idx[eng] += 1
            semname = f"{eng}_d{qi % self.NDMASEM}"
        else:
            semname = eng
        waits = {}

        def need(oid):
            if oid is None:
                return
            s2 = self.sem_of[oid]
            if s2 == eng == "tensor":
                return
            if self.hwm[eng].get(s2, -1) >= oid:
                return
            if waits.get(s2, -1) < oid:
                waits[s2] = oid

        rkeys = [k for it in reads for k in _keys(it)]
        wkeys = [k for it in writes for k in _keys(it)]
        for k in rkeys:
            rec = self.res.get(k)
            if rec:
                need(rec["w"])
            if self.rr_edges and k.startswith("p") and eng in ("vector", "scalar"):
                other = "scalar" if eng == "vector" else "vector"
                lr = self.psum_reads.get(k)
                if lr and lr[0] == other:
                    need(lr[1])
        for k in wkeys:
            rec = self.res.get(k)
            if rec:
                need(rec["w"])
                for oid in rec["r"]:
                    need(oid)
        if dma:
            need(self.last_dma_on_sem.get(semname))
            self.last_dma_on_sem[semname] = opid
        if wait_all_of is not None:
            for s2, oid in self.last_dma_on_sem.items():
                if s2.startswith(wait_all_of):
                    need(oid)
        for s2, oid in waits.items():
            self.hwm[eng][s2] = max(self.hwm[eng].get(s2, -1), oid)
        self.eng_of.append(eng)
        self.sem_of.append(semname)
        self.ops.append({"eng": eng, "emit": emit, "waits": waits, "dma": dma})
        for k in rkeys:
            self.res.setdefault(k, {"w": None, "r": []})["r"].append(opid)
            if k.startswith("p") and eng in ("vector", "scalar"):
                self.psum_reads[k] = (eng, opid)
        for k in wkeys:
            self.res[k] = {"w": opid, "r": []}
        return opid

    def sem_names(self):
        names = list(ENGINES)
        for e in ("sync", "gpsimd"):
            names += [f"{e}_d{i}" for i in range(self.NDMASEM)]
        return names

    def emit(self, block, sems):
        referenced = set()
        for o in self.ops:
            referenced.update(o["waits"].values())
        count = {s: 0 for s in self.sem_names()}
        val = [0] * len(self.ops)
        incs = [0] * len(self.ops)
        for opid, o in enumerate(self.ops):
            s = self.sem_of[opid]
            inc = 16 if o["dma"] else (1 if opid in referenced else 0)
            incs[opid] = inc
            count[s] += inc
            val[opid] = count[s]
        streams = {e: [] for e in ENGINES}
        for opid, o in enumerate(self.ops):
            wv = {s2: val[oid] for s2, oid in o["waits"].items()}
            streams[o["eng"]].append((o["emit"], wv, incs[opid], self.sem_of[opid]))

        def run(eng_name, eng):
            for emit_fn, wv, inc, semname in streams[eng_name]:
                for s2 in sorted(wv):
                    eng.wait_ge(sems[s2], wv[s2])
                inst = emit_fn(eng)
                if inc:
                    inst.then_inc(sems[semname], inc)

        @block.sync
        def _(sync):
            run("sync", sync)

        @block.gpsimd
        def _(gpsimd):
            run("gpsimd", gpsimd)

        @block.tensor
        def _(tensor):
            run("tensor", tensor)

        @block.vector
        def _(vector):
            run("vector", vector)

        @block.scalar
        def _(scalar):
            run("scalar", scalar)


def build_program(phases=4, asub=5, npairs=NP, rsub=5, dbg=False, nexp=NE, tdbg=0):
    nc = bass.Bass("TRN2", target_bir_lowering=False)

    xT = nc.dram_tensor("xT", [D, L], F32, kind="ExternalInput").ap()
    wq_d = nc.dram_tensor("wq", [D, D], F32, kind="ExternalInput").ap()
    wvk_d = nc.dram_tensor("wvk", [D, 2 * D], F32, kind="ExternalInput").ap()
    wout_d = nc.dram_tensor("wout", [D, D], F32, kind="ExternalInput").ap()
    wrt_d = nc.dram_tensor("wrt", [D, 16], F32, kind="ExternalInput").ap()
    rbias_d = nc.dram_tensor("rbias", [1, 16], F32, kind="ExternalInput").ap()
    gnw_d = nc.dram_tensor("gnw", [128, 8], F32, kind="ExternalInput").ap()
    gnb_d = nc.dram_tensor("gnb", [128, 8], F32, kind="ExternalInput").ap()
    wg_d = nc.dram_tensor("wg", [NE, D, 512], F32, kind="ExternalInput").ap()
    wu_d = nc.dram_tensor("wu", [NE, D, 512], F32, kind="ExternalInput").ap()
    wd_d = nc.dram_tensor("wd", [NE, 512, D], F32, kind="ExternalInput").ap()
    dloc_d = nc.dram_tensor("dloc", [NP, 128, 256], F32, kind="ExternalInput").ap()
    qds_d = nc.dram_tensor("qds", [NP, 128, 128], F32, kind="ExternalInput").ap()
    kdv_d = nc.dram_tensor("kdv", [128, NH], F32, kind="ExternalInput").ap()
    g128_d = nc.dram_tensor("g128", [128, NP], F32, kind="ExternalInput").ap()
    ident_d = nc.dram_tensor("ident", [128, 128], F32, kind="ExternalInput").ap()
    out_d = nc.dram_tensor("out", [D, L], F32, kind="ExternalOutput").ap()
    if dbg:
        dbg_xn = nc.dram_tensor("dbg_xn", [D, L], F32, kind="ExternalOutput").ap()
        dbg_s = nc.dram_tensor("dbg_s", [D, L], F32, kind="ExternalOutput").ap()
        dbg_y = nc.dram_tensor("dbg_y", [D, L], F32, kind="ExternalOutput").ap()
        dbg_comb = nc.dram_tensor("dbg_comb", [128, 256], F32, kind="ExternalOutput").ap()
        dbg_cmb = nc.dram_tensor("dbg_cmb", [D, L], F32, kind="ExternalOutput").ap()
    lnrows_d = nc.dram_tensor("lnrows", [1, 2 * L], F32, kind="Internal").ap()
    combt_d = nc.dram_tensor("combt", [256, 128], F32, kind="Internal").ap()

    b = Builder()

    import contextlib
    with contextlib.ExitStack() as _es:
        AR = _es.enter_context(nc.sbuf_tensor("arena", [128, ARENA_W], F32))
        PS = _es.enter_context(nc.psum_tensor("parena", [128, 4096], F32))
        sems = {s: _es.enter_context(nc.semaphore(f"s_{s}")) for s in b.sem_names()}
        block = _es.enter_context(nc.Block())

        def a_f32(off, w):
            return AR[:, off:off + w]

        def a_bf(off, w):
            return AR[:, off:off + w].bitcast(BF16)

        XN = 0
        SBF = 8192
        P2 = 16384
        OV = 17024
        UP = OV + 27136

        xn_t = [a_bf(XN + k * 1024, 1024) for k in range(NDT)]
        s_sb = [a_bf(SBF + k * 1024, 1024) for k in range(NDT)]
        mrow = a_f32(SBF + 0, 2048)
        msqrow = a_f32(SBF + 2048, 2048)
        xsq = [a_f32(SBF + 4096 + i * 2048, 2048) for i in range(2)]
        m2row = a_f32(UP + 0, 2048)

        comb = a_f32(P2, 256)
        gnw = a_f32(P2 + 256, 8)
        gnb = a_f32(P2 + 264, 8)
        kdvec = a_f32(P2 + 272, 16)
        onesrow = a_f32(P2 + 288, 128)
        sones_f = a_f32(P2 + 416, 1)
        sones_b = a_bf(P2 + 417, 1)[:, 0:1]
        gones_f = a_f32(P2 + 418, 1)
        rbias_sb = a_f32(P2 + 420, 16)
        wrt_sb = [a_f32(P2 + 436 + k * 16, 16) for k in range(NDT)]
        GNS = P2 + 564
        st_pair = a_f32(GNS, 2)
        gn_row = a_f32(GNS + 2, 4)
        gn_row2 = a_f32(GNS + 6, 4)
        rstdv = a_f32(GNS + 10, 1)
        mrv = a_f32(GNS + 11, 1)
        scalev = a_f32(GNS + 12, 1)
        biasv = a_f32(GNS + 13, 1)
        gn_t1 = a_f32(GNS + 14, 1)

        x32 = [a_f32(OV + k * 2048, 2048) for k in range(NDT)]

        CMB = OV + 0
        WOUTS = OV + 8192
        O0, O1 = OV + 12288, OV + 14336
        QT, KT, VT, KDT = OV + 16384, OV + 17408, OV + 18432, OV + 19456
        WPQ, WPVK = OV + 20480, OV + 21504
        DLOC, QDS_ = OV + 23552, OV + 24576
        APBF = OV + 25088
        RINCS = OV + 27136
        RBFS = OV + 27648
        qt = a_bf(QT, 1024)
        kt = a_bf(KT, 1024)
        vt = a_bf(VT, 1024)
        kdt = a_bf(KDT, 1024)
        opair = [a_f32(O0, 2048), a_f32(O1, 2048)]
        apbf = a_bf(APBF, 2048)
        rincs = a_bf(RINCS, 512)
        rbfs = a_bf(RBFS, 512)
        cmb_t = [a_bf(CMB + k * 1024, 1024) for k in range(NDT)]
        wpq = [a_bf(WPQ + i * 512, 512) for i in range(2)]
        wpvk = [a_bf(WPVK + i * 1024, 1024) for i in range(2)]
        dloc = [a_bf(DLOC + hp * 128, 128) for hp in range(NP)]
        qds = [a_bf(QDS_ + hp * 64, 64) for hp in range(NP)]
        g128v = a_f32(P2 + 580, 8)
        wouts = [a_bf(WOUTS + k * 512, 512) for k in range(NDT)]

        YS = OV
        WST = OV + 16384
        SG = OV + 28672
        HB = OV + 30720
        CBB = OV + 34816
        FIN = WST
        y_all = a_f32(YS, 16384)
        y_t = [a_f32(YS + k * 2048, 2048) for k in range(NDT)]
        wgt = [a_bf(WST + i * 6144, 2048) for i in range(2)]
        wut = [a_bf(WST + i * 6144 + 2048, 2048) for i in range(2)]
        wdt = [a_bf(WST + i * 6144 + 4096, 2048) for i in range(2)]
        sgf = a_f32(SG, 2048)
        hb_t = [a_bf(HB + i * 1024, 1024) for i in range(4)]
        combB = a_bf(CBB, 1024)
        xfin = [a_f32(FIN + i * 2048, 2048) for i in range(2)]
        t1f = [a_f32(FIN + 4096 + i * 2048, 2048) for i in range(2)]
        ofin = [a_f32(FIN + 8192 + i * 2048, 2048) for i in range(2)]

        rstdB = a_f32(UP, 2048)
        mrstdB = a_f32(UP + 2048, 2048)
        ident = a_f32(UP + 4096, 128)
        raw = a_f32(UP + 4224, 256)
        mtok = a_f32(UP + 4480, 16)
        qtok = a_f32(UP + 4496, 16)
        vtok = a_f32(UP + 4512, 16)
        rstdt = a_f32(UP + 4528, 16)
        colsB = a_f32(UP + 4544, 16)
        biasB = a_f32(UP + 4560, 16)
        T2 = a_f32(UP + 4576, 64)
        GR = a_f32(UP + 4640, 64)
        M9 = a_f32(UP + 4704, 64)
        GM = a_f32(UP + 4768, 64)
        MK = a_f32(UP + 4864, 256)
        CE = a_f32(UP + 5120, 256)
        RK = a_f32(UP + 5376, 256)
        WW = a_f32(UP + 5632, 256)
        WS = a_f32(UP + 5888, 16)
        SCR = a_f32(UP + 5904, 512)
        combT = a_f32(UP + 6416, 256)
        qdfull = a_bf(UP + 6672, 1024)
        VTF = UP + 7696
        IDB = UP + 8720
        vtf = a_bf(VTF, 1024)
        identb = a_bf(IDB, 64)

        q = b.op
        rA = lambda off, n: A(off, n)

        q("sync", lambda e: e.dma_start(out=gnw, in_=gnw_d), writes=[A(P2 + 256, 8)], dma=True)
        q("sync", lambda e: e.dma_start(out=gnb, in_=gnb_d), writes=[A(P2 + 264, 8)], dma=True)
        q("sync", lambda e: e.dma_start(out=kdvec, in_=kdv_d), writes=[A(P2 + 272, 16)], dma=True)
        q("sync", lambda e: e.dma_start(out=rbias_sb[0:1, :], in_=rbias_d),
          writes=[A(P2 + 420, 16)], dma=True)
        q("sync", lambda e: e.dma_start(
            out=AR[:, P2 + 436:P2 + 436 + 128].rearrange("p (k c) -> p k c", c=16),
            in_=wrt_d.rearrange("(k p) c -> p k c", p=128)),
          writes=[A(P2 + 436, 128)], dma=True)
        q("sync", lambda e: e.dma_start(out=ident, in_=ident_d),
          writes=[A(UP + 4096, 128)], dma=True)
        q("gpsimd", lambda e: e.dma_start(out=identb, in_=ident_d),
          writes=[A(IDB, 64)], dma=True)
        q("vector", lambda e: e.memset(onesrow[0:1, :], 1.0), writes=[A(P2 + 288, 128)])
        q("vector", lambda e: e.memset(sones_f, 1.0 / 1024.0), writes=[A(P2 + 416, 1)])
        q("vector", lambda e: e.memset(sones_b, 1.0 / 1024.0), writes=[A(P2 + 417, 1)])
        q("vector", lambda e: e.memset(gones_f, 1.0 / 131072.0), writes=[A(P2 + 418, 1)])
        for k in range(NDT):
            q("sync", lambda e, k=k: e.dma_start(out=x32[k], in_=xT[k * 128:(k + 1) * 128, :]),
              writes=[A(OV + k * 2048, 2048)], dma=True)

        X3K = lambda k: A(OV + k * 2048, 2048)
        XSK = lambda d: A(SBF + 4096 + (d % 2) * 2048, 2048)
        for tc in range(4):
            for d in range(NDT):
                q("tensor", lambda e, tc=tc, d=d: e.matmul(
                    PS[0:1, tc * 512:(tc + 1) * 512], sones_f,
                    x32[d][:, tc * 512:(tc + 1) * 512], start=(d == 0), stop=(d == NDT - 1)),
                  reads=[X3K(d), A(P2 + 416, 1)], writes=[P(tc * 512, 512)])
        q("vector", lambda e: e.tensor_copy(mrow[0:1, :], PS[0:1, 0:2048]),
          reads=[P(0, 2048)], writes=[A(SBF, 2048)])
        for d in range(NDT):
            q("scalar", lambda e, d=d: e.activation(xsq[d % 2], x32[d], AF.Square),
              reads=[X3K(d)], writes=[XSK(d)])
            for tc in range(4):
                q("tensor", lambda e, tc=tc, d=d: e.matmul(
                    PS[0:1, 2048 + tc * 512:2048 + (tc + 1) * 512], sones_f,
                    xsq[d % 2][:, tc * 512:(tc + 1) * 512], start=(d == 0), stop=(d == NDT - 1)),
                  reads=[XSK(d), A(P2 + 416, 1)],
                  writes=[P(2048 + tc * 512, 512)])
        q("vector", lambda e: e.tensor_copy(msqrow[0:1, :], PS[0:1, 2048:4096]),
          reads=[P(2048, 2048)], writes=[A(SBF + 2048, 2048)])
        for d in range(NDT):
            for tc in range(4):
                q("tensor", lambda e, tc=tc, d=d: e.matmul(
                    PS[0:16, tc * 512:(tc + 1) * 512], wrt_sb[d],
                    x32[d][:, tc * 512:(tc + 1) * 512],
                    start=(d == 0), stop=(d == NDT - 1)),
                  reads=[X3K(d), A(P2 + 436, 128)], writes=[P(tc * 512, 512)])
        rawfm = AR[0:16, SBF + 6144:SBF + 8192]
        q("vector", lambda e: e.tensor_copy(rawfm, PS[0:16, 0:2048]),
          reads=[P(0, 2048)], writes=[A(SBF + 6144, 2048)])
        for c in range(NC):
            q("tensor", lambda e, c=c: e.transpose(
                PS[:, 1024 + c * 16:1024 + (c + 1) * 16],
                rawfm[:, c * 128:(c + 1) * 128], ident[0:16, 0:16]),
              reads=[A(SBF + 6144, 2048), A(UP + 4096, 128)], writes=[P(1024, 256)])
        q("vector", lambda e: e.tensor_copy(raw, PS[:, 1024:1280]),
          reads=[P(1024, 256)], writes=[A(UP + 4224, 256)])
        for d in range(NDT):
            q("tensor", lambda e, d=d: e.matmul(
                PS[0:1, 1536:1552], sones_f, wrt_sb[d], start=(d == 0), stop=(d == NDT - 1)),
              reads=[A(P2 + 436, 128), A(P2 + 416, 1)], writes=[P(1536, 16)])
        q("vector", lambda e: e.tensor_copy(SCR[0:1, 0:16], PS[0:1, 1536:1552]),
          reads=[P(1536, 16)], writes=[A(UP + 5904, 16)])
        q("tensor", lambda e: e.matmul(PS[:, 1552:1568], onesrow[0:1, :], SCR[0:1, 0:16],
                                       start=True, stop=True),
          reads=[A(UP + 5904, 16), A(P2 + 288, 128)], writes=[P(1536, 32)])
        q("vector", lambda e: e.tensor_scalar(out=colsB, in0=PS[:, 1552:1568], scalar1=1024.0,
                                              scalar2=None, op0=ALU.mult),
          reads=[P(1536, 32)], writes=[A(UP + 4544, 16)])
        q("tensor", lambda e: e.matmul(PS[:, 1568:1584], onesrow[0:1, :], rbias_sb[0:1, :],
                                       start=True, stop=True),
          reads=[A(P2 + 420, 16), A(P2 + 288, 128)], writes=[P(1536, 48)])
        q("vector", lambda e: e.tensor_copy(biasB, PS[:, 1568:1584]),
          reads=[P(1536, 48)], writes=[A(UP + 4560, 16)])

        q("vector", lambda e: e.tensor_tensor(m2row[0:1, :], mrow[0:1, :], mrow[0:1, :],
                                              op=ALU.mult),
          reads=[A(SBF, 2048)], writes=[A(UP, 2048)])
        q("vector", lambda e: e.tensor_tensor(msqrow[0:1, :], msqrow[0:1, :], m2row[0:1, :],
                                              op=ALU.subtract),
          reads=[A(SBF + 2048, 2048), A(UP, 2048)], writes=[A(SBF + 2048, 2048)])
        q("vector", lambda e: e.tensor_scalar(out=msqrow[0:1, :], in0=msqrow[0:1, :],
                                              scalar1=EPS, scalar2=None, op0=ALU.add),
          reads=[A(SBF + 2048, 2048)], writes=[A(SBF + 2048, 2048)])
        q("scalar", lambda e: e.activation(msqrow[0:1, :], msqrow[0:1, :], AF.Sqrt),
          reads=[A(SBF + 2048, 2048)], writes=[A(SBF + 2048, 2048)])
        q("vector", lambda e: e.reciprocal(msqrow[0:1, :], msqrow[0:1, :]),
          reads=[A(SBF + 2048, 2048)], writes=[A(SBF + 2048, 2048)])
        q("vector", lambda e: e.tensor_tensor(mrow[0:1, :], mrow[0:1, :], msqrow[0:1, :],
                                              op=ALU.mult),
          reads=[A(SBF, 2048), A(SBF + 2048, 2048)], writes=[A(SBF, 2048)])
        q("sync", lambda e: e.dma_start(out=lnrows_d[0:1, 0:L], in_=msqrow[0:1, :]),
          reads=[A(SBF + 2048, 2048)], writes=["lnra"], dma=True)
        q("sync", lambda e: e.dma_start(out=lnrows_d[0:1, L:2 * L], in_=mrow[0:1, :]),
          reads=[A(SBF, 2048)], writes=["lnrb"], dma=True)
        for half in range(2):
            bsrc = msqrow if half == 0 else mrow
            skey = A(SBF + 2048, 2048) if half == 0 else A(SBF, 2048)
            dst = rstdB if half == 0 else mrstdB
            dkey = A(UP, 2048) if half == 0 else A(UP + 2048, 2048)
            pb = 2048 if half == 0 else 0
            for tc in range(4):
                q("tensor", lambda e, bsrc=bsrc, tc=tc, pb=pb: e.matmul(
                    PS[:, pb + tc * 512:pb + (tc + 1) * 512], onesrow[0:1, :],
                    bsrc[0:1, tc * 512:(tc + 1) * 512], start=True, stop=True),
                  reads=[skey, A(P2 + 288, 128)], writes=[P(pb + tc * 512, 512)])
            for tc in range(4):
                q("scalar", lambda e, dst=dst, tc=tc, pb=pb: e.copy(
                    dst[:, tc * 512:(tc + 1) * 512], PS[:, pb + tc * 512:pb + (tc + 1) * 512]),
                  reads=[P(pb + tc * 512, 512)], writes=[dkey])
        sc16a = AR[0:16, UP + 5904 + 128:UP + 5904 + 256]
        sc16b = AR[0:16, UP + 5904 + 256:UP + 5904 + 384]
        q("sync", lambda e: e.dma_start(
            out=sc16a, in_=lnrows_d[0, 0:L].rearrange("(c p) -> c p", p=128)),
          reads=["lnra"], writes=[A(UP + 6032, 128)], dma=True)
        q("sync", lambda e: e.dma_start(
            out=sc16b, in_=lnrows_d[0, L:2 * L].rearrange("(c p) -> c p", p=128)),
          reads=["lnrb"], writes=[A(UP + 6160, 128)], dma=True)
        q("tensor", lambda e: e.transpose(PS[:, 3584:3600], sc16a, ident[0:16, 0:16]),
          reads=[A(UP + 6032, 128), A(UP + 4096, 128)], writes=[P(3584, 16)])
        q("tensor", lambda e: e.transpose(PS[:, 3600:3616], sc16b, ident[0:16, 0:16]),
          reads=[A(UP + 6160, 128), A(UP + 4096, 128)], writes=[P(3584, 32)])
        q("scalar", lambda e: e.copy(rstdt, PS[:, 3584:3600]),
          reads=[P(3584, 16)], writes=[A(UP + 4528, 16)])
        q("scalar", lambda e: e.copy(mtok, PS[:, 3600:3616]),
          reads=[P(3584, 32)], writes=[A(UP + 4480, 16)])

        for d in range(NDT):
            eng = "vector" if d % 2 == 0 else "gpsimd"
            q(eng, lambda e, d=d: e.tensor_tensor(x32[d], x32[d], rstdB, op=ALU.mult),
              reads=[X3K(d), A(UP, 2048)], writes=[X3K(d)])
            q(eng, lambda e, d=d: e.tensor_tensor(xn_t[d], x32[d], mrstdB, op=ALU.subtract),
              reads=[X3K(d), A(UP + 2048, 2048)], writes=[A(XN + d * 1024, 1024)])

        routing_thunks = []

        def qr(eng, emit, reads=(), writes=()):
            routing_thunks.append(lambda: q(eng, emit, reads=reads, writes=writes))

        def drain_routing(k):
            for _ in range(k):
                if routing_thunks:
                    routing_thunks.pop(0)()

        def emit_routing():
            raw3 = raw.rearrange("p (c e) -> p c e", e=16)
            qr("vector", lambda e: e.tensor_tensor(
                CE, mtok[:, :, None].broadcast_to([128, 16, 16]),
                colsB[:, None, :].broadcast_to([128, 16, 16]), op=ALU.mult),
                reads=[A(UP + 4480, 16), A(UP + 4544, 16)], writes=[A(UP + 5120, 256)])
            qr("vector", lambda e: e.tensor_tensor(
                raw3, raw3, rstdt[:, :, None].broadcast_to([128, 16, 16]), op=ALU.mult),
                reads=[A(UP + 4224, 256), A(UP + 4528, 16)], writes=[A(UP + 4224, 256)])
            qr("vector", lambda e: e.tensor_tensor(raw, raw, CE, op=ALU.subtract),
                reads=[A(UP + 4224, 256), A(UP + 5120, 256)], writes=[A(UP + 4224, 256)])
            qr("scalar", lambda e: e.activation(raw, raw, AF.Sigmoid),
                reads=[A(UP + 4224, 256)], writes=[A(UP + 4224, 256)])
            qr("vector", lambda e: e.tensor_tensor(
                MK.rearrange("p (c e) -> p c e", e=16), raw3,
                biasB[:, None, :].broadcast_to([128, 16, 16]), op=ALU.add),
                reads=[A(UP + 4224, 256), A(UP + 4560, 16)], writes=[A(UP + 4864, 256)])
            pair_list = [(0, 1), (0, 2), (0, 3), (1, 2), (1, 3), (2, 3)]
            for g in range(4):
                for pi, (i, j) in enumerate(pair_list):
                    ei, ej = 4 * g + i, 4 * g + j
                    qr("vector", lambda e, ei=ei, ej=ej: e.tensor_tensor(
                        CE[:, 0:256:16], MK[:, ei:256:16], MK[:, ej:256:16], op=ALU.add),
                        reads=[A(UP + 4864, 256)], writes=[A(UP + 5120, 256)])
                    if pi == 0:
                        qr("vector", lambda e, g=g: e.tensor_copy(T2[:, g:64:4], CE[:, 0:256:16]),
                            reads=[A(UP + 5120, 256)], writes=[A(UP + 4576, 64)])
                    else:
                        qr("vector", lambda e, g=g: e.tensor_tensor(
                            T2[:, g:64:4], T2[:, g:64:4], CE[:, 0:256:16], op=ALU.max),
                            reads=[A(UP + 5120, 256), A(UP + 4576, 64)], writes=[A(UP + 4576, 64)])
            for g in range(4):
                others = [gg for gg in range(4) if gg != g]
                for oi, gg in enumerate(others):
                    qr("vector", lambda e, g=g, gg=gg: e.tensor_tensor(
                        CE[:, 0:64:4], T2[:, gg:64:4], T2[:, g:64:4], op=ALU.is_gt),
                        reads=[A(UP + 4576, 64)], writes=[A(UP + 5120, 64)])
                    if oi == 0:
                        qr("vector", lambda e, g=g: e.tensor_copy(GR[:, g:64:4], CE[:, 0:64:4]),
                            reads=[A(UP + 5120, 64)], writes=[A(UP + 4640, 64)])
                    else:
                        qr("vector", lambda e, g=g: e.tensor_tensor(
                            GR[:, g:64:4], GR[:, g:64:4], CE[:, 0:64:4], op=ALU.add),
                            reads=[A(UP + 5120, 64), A(UP + 4640, 64)], writes=[A(UP + 4640, 64)])
            qr("vector", lambda e: e.tensor_scalar(out=GM, in0=GR, scalar1=1.5, scalar2=None,
                                                   op0=ALU.is_lt),
                reads=[A(UP + 4640, 64)], writes=[A(UP + 4768, 64)])
            qr("vector", lambda e: e.tensor_scalar(out=M9, in0=GM, scalar1=1.0, scalar2=1e9,
                                                   op0=ALU.subtract, op1=ALU.mult),
                reads=[A(UP + 4768, 64)], writes=[A(UP + 4704, 64)])
            for e_ in range(16):
                g = e_ // 4
                qr("vector", lambda e, e_=e_, g=g: e.tensor_tensor(
                    MK[:, e_:256:16], MK[:, e_:256:16], GM[:, g:64:4], op=ALU.mult),
                    reads=[A(UP + 4864, 256), A(UP + 4768, 64)], writes=[A(UP + 4864, 256)])
                qr("vector", lambda e, e_=e_, g=g: e.tensor_tensor(
                    MK[:, e_:256:16], MK[:, e_:256:16], M9[:, g:64:4], op=ALU.add),
                    reads=[A(UP + 4864, 256), A(UP + 4704, 64)], writes=[A(UP + 4864, 256)])
            for e_ in range(16):
                qr("vector", lambda e, e_=e_: e.tensor_tensor(
                    CE.rearrange("p (c e) -> p c e", e=16),
                    MK.rearrange("p (c e) -> p c e", e=16),
                    MK[:, e_:256:16][:, :, None].broadcast_to([128, 16, 16]), op=ALU.is_gt),
                    reads=[A(UP + 4864, 256)], writes=[A(UP + 5120, 256)])
                qr("vector", lambda e, e_=e_: e.reduce_sum(
                    RK[:, e_:256:16], CE.rearrange("p (c e) -> p c e", e=16), axis=AX.X),
                    reads=[A(UP + 5120, 256)], writes=[A(UP + 5376, 256)])
            qr("vector", lambda e: e.tensor_scalar(out=RK, in0=RK, scalar1=3.5, scalar2=None,
                                                   op0=ALU.is_lt),
                reads=[A(UP + 5376, 256)], writes=[A(UP + 5376, 256)])
            qr("vector", lambda e: e.tensor_tensor(WW, raw, RK, op=ALU.mult),
                reads=[A(UP + 4224, 256), A(UP + 5376, 256)], writes=[A(UP + 5632, 256)])
            qr("vector", lambda e: e.reduce_sum(WS, WW.rearrange("p (c e) -> p c e", e=16),
                                                axis=AX.X),
                reads=[A(UP + 5632, 256)], writes=[A(UP + 5888, 16)])
            qr("vector", lambda e: e.tensor_scalar(out=WS, in0=WS, scalar1=1e-20, scalar2=None,
                                                   op0=ALU.add),
                reads=[A(UP + 5888, 16)], writes=[A(UP + 5888, 16)])
            qr("vector", lambda e: e.reciprocal(WS, WS),
                reads=[A(UP + 5888, 16)], writes=[A(UP + 5888, 16)])
            qr("vector", lambda e: e.tensor_scalar(out=WS, in0=WS, scalar1=2.5, scalar2=None,
                                                   op0=ALU.mult),
                reads=[A(UP + 5888, 16)], writes=[A(UP + 5888, 16)])
            qr("vector", lambda e: e.tensor_tensor(
                comb.rearrange("p (c e) -> p c e", e=16), WW.rearrange("p (c e) -> p c e", e=16),
                WS[:, :, None].broadcast_to([128, 16, 16]), op=ALU.mult),
                reads=[A(UP + 5632, 256), A(UP + 5888, 16)], writes=[A(P2, 256)])

        def emit_routing_tail():
            drain_routing(len(routing_thunks))
            q("tensor", lambda e: e.transpose(PS[:, 3584:3712], comb[:, 0:128], ident),
              reads=[A(P2, 256), A(UP + 4096, 128)], writes=[P(3584, 128)])
            q("scalar", lambda e: e.copy(combT[:, 0:128], PS[:, 3584:3712]),
              reads=[P(3584, 128)], writes=[A(UP + 6416, 128)])
            q("tensor", lambda e: e.transpose(PS[:, 3584:3712], comb[:, 128:256], ident),
              reads=[A(P2, 256), A(UP + 4096, 128)], writes=[P(3584, 128)])
            q("scalar", lambda e: e.copy(combT[:, 128:256], PS[:, 3584:3712]),
              reads=[P(3584, 128)], writes=[A(UP + 6544, 128)])
            q("sync", lambda e: e.dma_start(out=combt_d[0:128, :], in_=combT[:, 0:128]),
              reads=[A(UP + 6416, 128)], writes=["combt"], dma=True)
            q("sync", lambda e: e.dma_start(out=combt_d[128:256, :], in_=combT[:, 128:256]),
              reads=[A(UP + 6544, 128)], writes=["combt"], dma=True)


        XNK = lambda d: A(XN + d * 1024, 1024)
        if phases < 2:
            q("sync", lambda e: e.nop(), wait_all_of="sync")
            b.emit(block, sems)
            return nc

        def load_pair(hp):
            buf = hp % 2
            q("gpsimd", lambda e, hp=hp, buf=buf: e.dma_start(
                out=wpq[buf].rearrange("p (k c) -> p k c", c=128),
                in_=wq_d[:, hp * 128:(hp + 1) * 128].rearrange("(k p) c -> p k c", p=128)),
              writes=[A(WPQ + buf * 512, 512)], dma=True)
            q("gpsimd", lambda e, hp=hp, buf=buf: e.dma_start(
                out=wpvk[buf].rearrange("p (k c) -> p k c", c=256),
                in_=wvk_d[:, hp * 256:(hp + 1) * 256].rearrange("(k p) c -> p k c", p=128)),
              writes=[A(WPVK + buf * 1024, 1024)], dma=True)

        load_pair(0)
        load_pair(1)
        q("sync", lambda e: e.dma_start(out=g128v, in_=g128_d),
          writes=[A(P2 + 580, 8)], dma=True)
        for hp in range(NP):
            q("gpsimd", lambda e, hp=hp: e.dma_start(out=dloc[hp], in_=dloc_d[hp]),
              writes=[A(DLOC + hp * 128, 128)], dma=True)
            q("gpsimd", lambda e, hp=hp: e.dma_start(out=qds[hp], in_=qds_d[hp]),
              writes=[A(QDS_ + hp * 64, 64)], dma=True)
        q("gpsimd", lambda e: e.dma_start(
            out=AR[:, WOUTS:WOUTS + 4096].bitcast(BF16).rearrange("p (k c) -> p k c", c=1024),
            in_=wout_d.rearrange("(k p) c -> p k c", p=128)),
          writes=[A(WOUTS, 4096)], dma=True)

        def emit_gn_stats(hp):
            ob = hp % 2
            OKEY = A(O0 if ob == 0 else O1, 2048)
            q("vector", lambda e, ob=ob: e.reduce_sum(st_pair[:, 0:1], opair[ob], axis=AX.X),
              reads=[OKEY], writes=[A(GNS, 2)])
            q("scalar", lambda e, ob=ob: e.activation(
                AR[:, SBF:SBF + 2048], opair[ob], AF.Square),
              reads=[OKEY], writes=[A(SBF, 2048)])
            q("vector", lambda e: e.reduce_sum(st_pair[:, 1:2], AR[:, SBF:SBF + 2048],
                                               axis=AX.X),
              reads=[A(SBF, 2048)], writes=[A(GNS, 2)])

        def emit_gn_mm1(hp):
            for h01 in range(2):
                o64 = h01 * 64
                base = 3072 if h01 == 0 else 3584
                for s01 in range(2):
                    q("tensor", lambda e, o64=o64, s01=s01, base=base: e.matmul(
                        PS[0:1, base + s01:base + s01 + 1],
                        st_pair[o64:o64 + 64, s01:s01 + 1], gones_f[o64:o64 + 64, :],
                        start=True, stop=True),
                      reads=[A(GNS, 2), A(P2 + 418, 1)], writes=[P(base, 4)])

        def emit_gn(hp):
            ob = hp % 2
            OKEY = A(O0 if ob == 0 else O1, 2048)
            q("scalar", lambda e: e.copy(gn_row[0:1, 0:2], PS[0:1, 3072:3074]),
              reads=[P(3072, 4)], writes=[A(GNS + 2, 4)])
            q("scalar", lambda e: e.copy(gn_row[0:1, 2:4], PS[0:1, 3584:3586]),
              reads=[P(3584, 4)], writes=[A(GNS + 2, 4)])
            q("vector", lambda e: e.tensor_tensor(gn_row2[0:1, 0:4:2], gn_row[0:1, 0:4:2],
                                                  gn_row[0:1, 0:4:2], op=ALU.mult),
              reads=[A(GNS + 2, 4)], writes=[A(GNS + 6, 4)])
            q("vector", lambda e: e.tensor_tensor(gn_row2[0:1, 0:4:2], gn_row[0:1, 1:4:2],
                                                  gn_row2[0:1, 0:4:2], op=ALU.subtract),
              reads=[A(GNS + 2, 4), A(GNS + 6, 4)], writes=[A(GNS + 6, 4)])
            q("vector", lambda e: e.tensor_scalar(out=gn_row2[0:1, 0:4:2],
                                                  in0=gn_row2[0:1, 0:4:2], scalar1=EPS,
                                                  scalar2=None, op0=ALU.add),
              reads=[A(GNS + 6, 4)], writes=[A(GNS + 6, 4)])
            q("scalar", lambda e: e.activation(gn_row2[0:1, 0:4:2], gn_row2[0:1, 0:4:2],
                                               AF.Ln),
              reads=[A(GNS + 6, 4)], writes=[A(GNS + 6, 4)])
            q("vector", lambda e: e.tensor_scalar(out=gn_row2[0:1, 0:4:2],
                                                  in0=gn_row2[0:1, 0:4:2], scalar1=-0.5,
                                                  scalar2=None, op0=ALU.mult),
              reads=[A(GNS + 6, 4)], writes=[A(GNS + 6, 4)])
            q("scalar", lambda e: e.activation(gn_row2[0:1, 0:4:2], gn_row2[0:1, 0:4:2],
                                               AF.Exp),
              reads=[A(GNS + 6, 4)], writes=[A(GNS + 6, 4)])
            q("vector", lambda e: e.tensor_tensor(gn_row2[0:1, 1:4:2], gn_row[0:1, 0:4:2],
                                                  gn_row2[0:1, 0:4:2], op=ALU.mult),
              reads=[A(GNS + 2, 4), A(GNS + 6, 4)], writes=[A(GNS + 6, 4)])
            for h01 in range(2):
                o64 = h01 * 64
                q("tensor", lambda e, h01=h01, o64=o64: e.matmul(
                    PS[o64:o64 + 64, 3076:3077], onesrow[0:1, 0:64],
                    gn_row2[0:1, 2 * h01:2 * h01 + 1], start=True, stop=True),
                  reads=[A(GNS + 6, 4), A(P2 + 288, 128)], writes=[P(3076, 2)])
                q("tensor", lambda e, h01=h01, o64=o64: e.matmul(
                    PS[o64:o64 + 64, 3077:3078], onesrow[0:1, 0:64],
                    gn_row2[0:1, 2 * h01 + 1:2 * h01 + 2], start=True, stop=True),
                  reads=[A(GNS + 6, 4), A(P2 + 288, 128)], writes=[P(3076, 2)])
            q("scalar", lambda e: e.copy(rstdv, PS[:, 3076:3077]),
              reads=[P(3076, 2)], writes=[A(GNS + 10, 1)])
            q("scalar", lambda e: e.copy(mrv, PS[:, 3077:3078]),
              reads=[P(3076, 2)], writes=[A(GNS + 11, 1)])
            q("vector", lambda e, hp=hp: e.tensor_tensor(scalev, rstdv, gnw[:, hp:hp + 1],
                                                         op=ALU.mult),
              reads=[A(GNS + 10, 1), A(P2 + 256, 8)], writes=[A(GNS + 12, 1)])
            q("vector", lambda e, hp=hp: e.tensor_tensor(gn_t1, mrv, gnw[:, hp:hp + 1],
                                                         op=ALU.mult),
              reads=[A(GNS + 11, 1), A(P2 + 256, 8)], writes=[A(GNS + 14, 1)])
            q("vector", lambda e, hp=hp: e.tensor_tensor(biasv, gnb[:, hp:hp + 1], gn_t1,
                                                         op=ALU.subtract),
              reads=[A(GNS + 14, 1), A(P2 + 264, 8)], writes=[A(GNS + 13, 1)])
            q("vector", lambda e, hp=hp, ob=ob: e.scalar_tensor_tensor(
                out=cmb_t[hp], in0=opair[ob], scalar=scalev,
                in1=biasv.broadcast_to([128, 2048]), op0=ALU.mult, op1=ALU.add),
              reads=[OKEY, A(GNS + 12, 1), A(GNS + 13, 1)],
              writes=[A(CMB + hp * 1024, 1024)])


        for hp in range(npairs):
            buf = hp % 2
            if asub < 1:
                continue
            if hp >= 1 and asub >= 4:
                emit_gn_stats(hp - 1)
            KWQ = A(WPQ + buf * 512, 512)
            KWVK = A(WPVK + buf * 1024, 1024)
            for which in range(3):
                dest = (qt, kt, vtf)[which]
                dkey = (A(QT, 1024), A(KT, 1024), A(VTF, 1024))[which]
                base = (0, 2048, 0)[which]
                for d in range(NDT):
                    lh = (wpq[buf][:, d * 128:(d + 1) * 128] if which == 0
                          else wpvk[buf][:, d * 256 + 128:(d + 1) * 256] if which == 1
                          else wpvk[buf][:, d * 256:d * 256 + 128])
                    lk = KWQ if which == 0 else KWVK
                    for tc in range(4):
                        q("tensor", lambda e, lh=lh, tc=tc, d=d, base=base: e.matmul(
                            PS[:, base + tc * 512:base + (tc + 1) * 512], lh,
                            xn_t[d][:, tc * 512:(tc + 1) * 512],
                            start=(d == 0), stop=(d == NDT - 1)),
                          reads=[lk, XNK(d)], writes=[P(base + tc * 512, 512)])
                for tc in range(4):
                    q("scalar", lambda e, dest=dest, tc=tc, base=base: e.copy(
                        dest[:, tc * 512:(tc + 1) * 512], PS[:, base + tc * 512:base + (tc + 1) * 512]),
                      reads=[P(base + tc * 512, 512)], writes=[dkey])
            PSB = PS.bitcast(BF16)
            for c in range(NC if asub >= 2 else 0):
                pw = 1024 + (c % 2) * 512
                if not (tdbg & 2):
                    q("tensor", lambda e, c=c, pw=pw: e.transpose(
                        PSB[:, 2 * pw:2 * pw + 128], vtf[:, c * 128:(c + 1) * 128], identb),
                      reads=[A(VTF, 1024), A(IDB, 64)], writes=[P(pw, 64)])
                    q("scalar", lambda e, c=c, pw=pw: e.copy(
                        vt[:, c * 128:(c + 1) * 128], PSB[:, 2 * pw:2 * pw + 128]),
                      reads=[P(pw, 128)], writes=[A(VT, 1024)])
                if not (tdbg & 1):
                    pk = [(c % 2) * 512, 3072 + (c % 2) * 512]
                    for h01 in range(2):
                        q("tensor", lambda e, c=c, h01=h01, pko=pk[h01]: e.transpose(
                            PSB[:, 2 * pko:2 * pko + 64],
                            kt[h01 * 64:h01 * 64 + 64, c * 128:(c + 1) * 128],
                            identb[h01 * 64:h01 * 64 + 64, h01 * 64:h01 * 64 + 64]),
                          reads=[A(KT, 1024), A(IDB, 64)], writes=[P(pk[h01], 32)])
                    for h01 in range(2):
                        h = 2 * hp + h01
                        q("scalar", lambda e, c=c, h01=h01, h=h, pko=pk[h01]: e.activation(
                            kdt[:, c * 128 + h01 * 64:c * 128 + (h01 + 1) * 64],
                            PSB[:, 2 * pko:2 * pko + 64],
                            AF.Copy, scale=kdvec[:, h:h + 1]),
                          reads=[P(pk[h01], 32), A(P2 + 272, 16)], writes=[A(KDT, 1024)])
            if hp + 2 < npairs:
                load_pair(hp + 2)
            if hp == 0:
                emit_routing()
            else:
                if hp == 2 or (hp == npairs - 1 and npairs <= 2):
                    emit_routing_tail()
                if asub >= 4:
                    emit_gn_mm1(hp - 1)
            if asub < 3:
                continue
            q("vector", lambda e, hp=hp: e.tensor_tensor(
                qdfull.rearrange("p (c a) -> p c a", a=128),
                qt.rearrange("p (c a) -> p c a", a=128),
                qds[hp][:, None, :].broadcast_to([128, 16, 128]), op=ALU.mult),
              reads=[A(QT, 1024), A(QDS_ + hp * 64, 64)], writes=[A(UP + 6672, 1024)])
            if rsub < 2:
                continue
            q("vector", lambda e: e.memset(rbfs[:, 0:64], 0.0),
              writes=[A(RBFS, 32)])
            for c in range(NC):
                pa0 = 2048 + (c % 2) * 512
                pa1 = 1024 + (c % 2) * 512
                pr = (c % 2) * 512
                for h01 in range(2):
                    o64 = h01 * 64
                    pa = pa0 if h01 == 0 else pa1
                    q("tensor", lambda e, c=c, h01=h01, o64=o64, pa=pa: e.matmul(
                        PS[:, pa:pa + 128],
                        kt[o64:o64 + 64, c * 128:(c + 1) * 128],
                        qt[o64:o64 + 64, c * 128:(c + 1) * 128], start=True, stop=True,
                        tile_position=(o64, 0)),
                      reads=[A(KT, 1024), A(QT, 1024)], writes=[P(pa, 128)])
                    q("vector", lambda e, c=c, hp=hp, pa=pa, h01=h01: e.tensor_tensor(
                        apbf[:, c * 256 + h01 * 128:c * 256 + (h01 + 1) * 128],
                        PS[:, pa:pa + 128],
                        dloc[hp][:, h01 * 128:(h01 + 1) * 128], op=ALU.mult),
                      reads=[P(pa, 128), A(DLOC + hp * 128, 128)],
                      writes=[A(APBF + c * 128, 128)])
                if rsub >= 4:
                    q("tensor", lambda e, c=c, pr=pr: e.matmul(
                        PS[:, pr + 256:pr + 384],
                        kdt[:, c * 128:(c + 1) * 128],
                        vt[:, c * 128:(c + 1) * 128], start=True, stop=True),
                      reads=[A(KDT, 1024), A(VT, 1024)], writes=[P(pr + 256, 128)])
                    for h01 in range(2):
                        o64 = h01 * 64
                        q("scalar", lambda e, c=c, pr=pr, o64=o64: e.copy(
                            rincs[o64:o64 + 64, c * 64:(c + 1) * 64],
                            PS[o64:o64 + 64, pr + 256 + o64:pr + 320 + o64]),
                          reads=[P(pr + 256, 128)], writes=[A(RINCS + c * 32, 32)])
                    if c < NC - 1:
                        q("vector", lambda e, c=c, hp=hp: e.scalar_tensor_tensor(
                            out=rbfs[:, (c + 1) * 64:(c + 2) * 64],
                            in0=rbfs[:, c * 64:(c + 1) * 64],
                            scalar=g128v[:, hp:hp + 1],
                            in1=rincs[:, c * 64:(c + 1) * 64],
                            op0=ALU.mult, op1=ALU.add),
                          reads=[A(RBFS + c * 32, 32), A(RINCS + c * 32, 32),
                                 A(P2 + 580, 8)],
                          writes=[A(RBFS + (c + 1) * 32, 32)])
                drain_routing(5)
            if hp >= 1 and asub >= 4:
                emit_gn(hp - 1)
            ob = hp % 2
            for c in range(NC):
                po = 3072 + (c % 2) * 512
                for h01 in range(2):
                    o64 = h01 * 64
                    q("tensor", lambda e, c=c, h01=h01, o64=o64, po=po: e.matmul(
                        PS[o64:o64 + 64, po:po + 128],
                        vt[:, c * 128 + o64:c * 128 + o64 + 64],
                        apbf[:, c * 256 + h01 * 128:c * 256 + (h01 + 1) * 128],
                        start=True, stop=(c == 0 or rsub < 5)),
                      reads=[A(VT, 1024), A(APBF + c * 128, 128)],
                      writes=[P(po, 128)])
                    if c > 0 and rsub >= 5:
                        q("tensor", lambda e, c=c, o64=o64, po=po: e.matmul(
                            PS[o64:o64 + 64, po:po + 128],
                            rbfs[o64:o64 + 64, c * 64:(c + 1) * 64],
                            qdfull[o64:o64 + 64, c * 128:(c + 1) * 128],
                            start=False, stop=True),
                          reads=[A(RBFS + c * 32, 32), A(UP + 6672, 1024)],
                          writes=[P(po, 128)])
                q("scalar", lambda e, c=c, po=po, ob=ob: e.copy(
                    opair[ob][:, c * 128:(c + 1) * 128], PS[:, po:po + 128]),
                  reads=[P(po, 128)], writes=[A(O0 if ob == 0 else O1, 2048)])
                drain_routing(3)

        if asub >= 4 and npairs > 0:
            emit_gn_stats(npairs - 1)
            emit_gn_mm1(npairs - 1)
            emit_gn(npairs - 1)

        for ot in range(NDT if asub >= 5 else 0):
            po = (ot % 2) * 2048
            for d in range(NDT):
                for tc in range(4):
                    q("tensor", lambda e, ot=ot, tc=tc, d=d, po=po: e.matmul(
                        PS[:, po + tc * 512:po + (tc + 1) * 512],
                        wouts[d][:, ot * 128:(ot + 1) * 128],
                        cmb_t[d][:, tc * 512:(tc + 1) * 512],
                        start=(d == 0), stop=(d == NDT - 1)),
                      reads=[A(WOUTS + d * 512, 512), A(CMB + d * 1024, 1024)],
                      writes=[P(po + tc * 512, 512)])
            for tc in range(4):
                q("scalar", lambda e, ot=ot, tc=tc, po=po: e.copy(
                    s_sb[ot][:, tc * 512:(tc + 1) * 512], PS[:, po + tc * 512:po + (tc + 1) * 512]),
                  reads=[P(po + tc * 512, 512)], writes=[A(SBF + ot * 1024, 1024)])

        if phases < 3:
            q("sync", lambda e: e.nop(), wait_all_of="sync")
            b.emit(block, sems)
            return nc

        def load_expert(ex):
            buf = ex % 2
            q("gpsimd", lambda e, ex=ex, buf=buf: e.dma_start(
                out=wgt[buf].rearrange("p (k c) -> p k c", c=512),
                in_=wg_d[ex].rearrange("(k p) c -> p k c", p=128)),
              writes=[A(WST + buf * 6144, 2048)], dma=True)
            q("gpsimd", lambda e, ex=ex, buf=buf: e.dma_start(
                out=wut[buf].rearrange("p (k c) -> p k c", c=512),
                in_=wu_d[ex].rearrange("(k p) c -> p k c", p=128)),
              writes=[A(WST + buf * 6144 + 2048, 2048)], dma=True)
            q("gpsimd", lambda e, ex=ex, buf=buf: e.dma_start(
                out=wdt[buf].rearrange("p (k c) -> p k c", c=1024),
                in_=wd_d[ex].rearrange("(k p) c -> p k c", p=128)),
              writes=[A(WST + buf * 6144 + 4096, 2048)], dma=True)

        def load_combB(ex):
            q("gpsimd", lambda e, ex=ex: e.dma_start(
                out=combB.rearrange("p (c t) -> p c t", t=128),
                in_=combt_d[ex:256:16, :].unsqueeze(0).broadcast_to([128, 16, 128])),
              reads=["combt"], writes=[A(CBB, 1024)], dma=True)

        b.rr_edges = False
        load_expert(0)
        load_combB(0)

        for ex in range(nexp):
            buf = ex % 2
            KWG = A(WST + buf * 6144, 2048)
            KWU = A(WST + buf * 6144 + 2048, 2048)
            KWD = A(WST + buf * 6144 + 4096, 2048)
            if ex + 1 < nexp:
                load_expert(ex + 1)
            for it in range(4):
                for d in range(NDT):
                    for tc in range(4):
                        q("tensor", lambda e, it=it, d=d, tc=tc, buf=buf: e.matmul(
                            PS[:, tc * 512:(tc + 1) * 512],
                            wgt[buf][:, d * 512 + it * 128:d * 512 + (it + 1) * 128],
                            xn_t[d][:, tc * 512:(tc + 1) * 512],
                            start=(d == 0), stop=(d == NDT - 1)),
                          reads=[KWG, XNK(d)], writes=[P(tc * 512, 512)])
                q("scalar", lambda e: e.activation(sgf, PS[:, 0:2048], AF.Silu),
                  reads=[P(0, 2048)], writes=[A(SG, 2048)])
                for d in range(NDT):
                    for tc in range(4):
                        q("tensor", lambda e, it=it, d=d, tc=tc, buf=buf: e.matmul(
                            PS[:, 2048 + tc * 512:2048 + (tc + 1) * 512],
                            wut[buf][:, d * 512 + it * 128:d * 512 + (it + 1) * 128],
                            xn_t[d][:, tc * 512:(tc + 1) * 512],
                            start=(d == 0), stop=(d == NDT - 1)),
                          reads=[KWU, XNK(d)], writes=[P(2048 + tc * 512, 512)])
                for tc in range(4):
                    q("vector", lambda e, tc=tc: e.tensor_tensor(
                        sgf[:, tc * 512:(tc + 1) * 512], sgf[:, tc * 512:(tc + 1) * 512],
                        combB[:, c2 * 512:(c2 + 1) * 512] if False else combB[:, tc * 512:(tc + 1) * 512],
                        op=ALU.mult),
                      reads=[A(SG + tc * 512, 512), A(CBB + tc * 256, 256)],
                      writes=[A(SG + tc * 512, 512)])
                for tc in range(4):
                    q("vector", lambda e, it=it, tc=tc: e.tensor_tensor(
                        hb_t[it][:, tc * 512:(tc + 1) * 512],
                        sgf[:, tc * 512:(tc + 1) * 512],
                        PS[:, 2048 + tc * 512:2048 + (tc + 1) * 512], op=ALU.mult),
                      reads=[P(2048 + tc * 512, 512), A(SG + tc * 512, 512)],
                      writes=[A(HB + it * 1024 + tc * 256, 256)])
            if ex + 1 < nexp:
                load_combB(ex + 1)
            last = (ex == nexp - 1) and phases >= 4
            if last:
                for k in range(2):
                    q("gpsimd", lambda e, k=k: e.dma_start(
                        out=xfin[k], in_=xT[k * 128:(k + 1) * 128, :]),
                      writes=[A(FIN + k * 2048, 2048)], dma=True)
            for ot in range(NDT):
                po = (ot % 2) * 2048
                for it in range(4):
                    for tc in range(4):
                        q("tensor", lambda e, ot=ot, it=it, tc=tc, buf=buf, po=po: e.matmul(
                            PS[:, po + tc * 512:po + (tc + 1) * 512],
                            wdt[buf][:, it * 1024 + ot * 128:it * 1024 + (ot + 1) * 128],
                            hb_t[it][:, tc * 512:(tc + 1) * 512],
                            start=(it == 0), stop=(it == 3)),
                          reads=[KWD, A(HB + it * 1024 + tc * 256, 256)],
                          writes=[P(po + tc * 512, 512)])
                if ex == 0:
                    q("vector", lambda e, ot=ot, po=po: e.tensor_copy(
                        y_t[ot], PS[:, po:po + 2048]),
                      reads=[P(po, 2048)], writes=[A(YS + ot * 2048, 2048)])
                else:
                    q("vector", lambda e, ot=ot, po=po: e.tensor_tensor(
                        y_t[ot], PS[:, po:po + 2048], y_t[ot], op=ALU.add),
                      reads=[P(po, 2048), A(YS + ot * 2048, 2048)],
                      writes=[A(YS + ot * 2048, 2048)])
                if last:
                    fb = ot % 2
                    q("scalar", lambda e, ot=ot: e.activation(xn_t[ot], y_t[ot], AF.Silu),
                      reads=[A(YS + ot * 2048, 2048)], writes=[XNK(ot)])
                    q("vector", lambda e, ot=ot, fb=fb: e.tensor_tensor(
                        t1f[fb], s_sb[ot], xn_t[ot], op=ALU.mult),
                      reads=[A(SBF + ot * 1024, 1024), XNK(ot)],
                      writes=[A(FIN + 4096 + fb * 2048, 2048)])
                    q("gpsimd", lambda e, ot=ot, fb=fb: e.tensor_tensor(
                        t1f[fb], t1f[fb], xfin[fb], op=ALU.add),
                      reads=[A(FIN + 4096 + fb * 2048, 2048), A(FIN + fb * 2048, 2048)],
                      writes=[A(FIN + 4096 + fb * 2048, 2048)])
                    q("sync", lambda e, ot=ot, fb=fb: e.dma_start(
                        out=out_d[ot * 128:(ot + 1) * 128, :], in_=t1f[fb]),
                      reads=[A(FIN + 4096 + fb * 2048, 2048)],
                      writes=[f"outd{ot}"], dma=True)
                    if ot + 2 < NDT:
                        q("gpsimd", lambda e, ot=ot, fb=fb: e.dma_start(
                            out=xfin[fb], in_=xT[(ot + 2) * 128:(ot + 3) * 128, :]),
                          writes=[A(FIN + fb * 2048, 2048)], dma=True)

        if dbg:
            for k in range(NDT):
                q("gpsimd", lambda e, k=k: e.dma_start(
                    out=dbg_xn[k * 128:(k + 1) * 128, :], in_=xn_t[k]),
                  reads=[A(XN + k * 1024, 1024)], writes=[f"dbgxn{k}"], dma=True)
                q("gpsimd", lambda e, k=k: e.dma_start(
                    out=dbg_s[k * 128:(k + 1) * 128, :], in_=s_sb[k]),
                  reads=[A(SBF + k * 1024, 1024)], writes=[f"dbgs{k}"], dma=True)
                q("gpsimd", lambda e, k=k: e.dma_start(
                    out=dbg_cmb[k * 128:(k + 1) * 128, :], in_=cmb_t[k]),
                  reads=[A(CMB + k * 1024, 1024)], writes=[f"dbgc{k}"], dma=True)
                if phases >= 3:
                    q("sync", lambda e, k=k: e.dma_start(
                        out=dbg_y[k * 128:(k + 1) * 128, :], in_=y_t[k]),
                      reads=[A(YS + k * 2048, 2048)], writes=[f"dbgy{k}"], dma=True)
            q("sync", lambda e: e.dma_start(out=dbg_comb, in_=comb),
              reads=[A(P2, 256)], writes=["dbgcomb"], dma=True)
            q("gpsimd", lambda e: e.nop(), wait_all_of="gpsimd")

        q("sync", lambda e: e.nop(), wait_all_of="sync")

        b.emit(block, sems)

    return nc


_NC_CACHE = {}


def _get_nc():
    if "nc" not in _NC_CACHE:
        _NC_CACHE["nc"] = build_program()
    return _NC_CACHE["nc"]


def _host_constants():
    dloc = np.zeros((NP, 128, 256), np.float64)
    qds = np.zeros((NP, 128, 128), np.float64)
    kdv = np.zeros((128, NH), np.float64)
    idx = np.arange(128)
    for hp in range(NP):
        for h01 in range(2):
            g = GAMMAS[2 * hp + h01]
            diff = idx[None, :] - idx[:, None]
            dl = np.where(diff >= 0, g ** np.maximum(diff, 0) / 8.0, 0.0)
            dloc[hp, :, h01 * 128:(h01 + 1) * 128] = dl
            qds[hp, h01 * 64:(h01 + 1) * 64, :] = (g ** (idx[None, :] + 1.0)) / 8.0
    for h in range(NH):
        kdv[:, h] = GAMMAS[h] ** (127.0 - idx)
    g128 = np.zeros((128, NP), np.float64)
    for hp in range(NP):
        g128[0:64, hp] = GAMMAS[2 * hp] ** 128
        g128[64:128, hp] = GAMMAS[2 * hp + 1] ** 128
    return (dloc.astype(np.float32), qds.astype(np.float32), kdv.astype(np.float32),
            g128.astype(np.float32))


def kernel(L0_attn_x, L0_attn_Wq, L0_attn_Wk, L0_attn_Wv, L0_attn_gn_weight,
           L0_attn_gn_bias, L0_attn_Wout, L0_ffn_router_weight, L0_ffn_router_bias,
           L0_ffn_w_gate, L0_ffn_w_up, L0_ffn_w_down):
    x = np.asarray(L0_attn_x)
    B = x.shape[0]
    wq_cat = np.ascontiguousarray(np.asarray(L0_attn_Wq).transpose(1, 0, 2).reshape(D, D))
    wk_cat = np.asarray(L0_attn_Wk).transpose(1, 0, 2).reshape(D, D)
    wv_cat = np.asarray(L0_attn_Wv).transpose(1, 0, 2).reshape(D, D)
    wvk = np.empty((D, 2 * D), np.float32)
    for hp in range(NP):
        wvk[:, hp * 256:hp * 256 + 128] = wv_cat[:, hp * 128:(hp + 1) * 128]
        wvk[:, hp * 256 + 128:hp * 256 + 256] = wk_cat[:, hp * 128:(hp + 1) * 128]
    dloc, qds, kdv, g128 = _host_constants()
    shared = {
        "wq": wq_cat.astype(np.float32, copy=False),
        "wvk": wvk,
        "wout": np.ascontiguousarray(np.asarray(L0_attn_Wout), dtype=np.float32),
        "wrt": np.ascontiguousarray(np.asarray(L0_ffn_router_weight).T, dtype=np.float32),
        "rbias": np.ascontiguousarray(np.asarray(L0_ffn_router_bias).reshape(1, 16),
                                      dtype=np.float32),
        "gnw": np.ascontiguousarray(np.asarray(L0_attn_gn_weight).reshape(8, 128).T,
                                    dtype=np.float32),
        "gnb": np.ascontiguousarray(np.asarray(L0_attn_gn_bias).reshape(8, 128).T,
                                    dtype=np.float32),
        "wg": np.ascontiguousarray(np.asarray(L0_ffn_w_gate).transpose(0, 2, 1),
                                   dtype=np.float32),
        "wu": np.ascontiguousarray(np.asarray(L0_ffn_w_up).transpose(0, 2, 1),
                                   dtype=np.float32),
        "wd": np.ascontiguousarray(np.asarray(L0_ffn_w_down).transpose(0, 2, 1),
                                   dtype=np.float32),
        "dloc": dloc, "qds": qds, "kdv": kdv, "g128": g128,
        "ident": np.eye(128, dtype=np.float32),
    }
    in_maps = []
    for bi in range(B):
        m = dict(shared)
        m["xT"] = np.ascontiguousarray(x[bi].T)
        in_maps.append(m)
    nc = _get_nc()
    res = run_bass_kernel_spmd(nc, in_maps, core_ids=list(range(B)))
    out = np.stack([np.asarray(r["out"]).T for r in res.results], axis=0)
    return np.ascontiguousarray(out.astype(np.float32))


# version 22
# speedup vs baseline: 1.0083x; 1.0083x over previous
import math
import numpy as np
import concourse.bass as bass
import concourse.mybir as mybir
import concourse.bass_utils as _bu
from concourse.bass_utils import run_bass_kernel_spmd

if not getattr(_bu, "_ldw_patched", False):
    _orig_run_command = _bu.run_command

    def _run_command_ldw(cmd, *a, **kw):
        try:
            cmd = ["--enable-ldw-opt=true" if c == "--enable-ldw-opt=false" else c
                   for c in cmd]
        except TypeError:
            pass
        return _orig_run_command(cmd, *a, **kw)

    _bu.run_command = _run_command_ldw
    _bu._ldw_patched = True


F32 = mybir.dt.float32
BF16 = mybir.dt.bfloat16
AF = mybir.ActivationFunctionType
ALU = mybir.AluOpType
AX = mybir.AxisListType

D = 1024
L = 2048
NH = 16
NP = 8
NC = 16
NE = 16
NDT = 8
EPS = 1e-5
GAMMAS = [1.0 - 2.0 ** (-5 - math.floor(8 * i / NH)) for i in range(NH)]

ARENA_W = 53120
ENGINES = ("sync", "gpsimd", "tensor", "vector", "scalar")


def A(off, n):
    return ("A", off, n)


def P(off, n):
    return ("P", off, n)


def _keys(item):
    if isinstance(item, str):
        return (item,)
    kind, off, n = item
    if kind == "A":
        return tuple(f"a{p}" for p in range(off // 128, (off + n - 1) // 128 + 1))
    return tuple(f"p{b}" for b in range(off // 512, (off + n - 1) // 512 + 1))


class Builder:

    NDMASEM = 4

    def __init__(self):
        self.ops = []
        self.res = {}
        self.eng_of = []
        self.sem_of = []
        self.psum_reads = {}
        self.rr_edges = True
        self.dma_idx = {"sync": 0, "gpsimd": 0}
        self.last_dma_on_sem = {}
        self.hwm = {e: {} for e in ENGINES}

    def op(self, eng, emit, reads=(), writes=(), dma=False, wait_all_of=None):
        opid = len(self.ops)
        if dma:
            qi = self.dma_idx[eng]
            self.dma_idx[eng] += 1
            semname = f"{eng}_d{qi % self.NDMASEM}"
        else:
            semname = eng
        waits = {}

        def need(oid):
            if oid is None:
                return
            s2 = self.sem_of[oid]
            if s2 == eng == "tensor":
                return
            if self.hwm[eng].get(s2, -1) >= oid:
                return
            if waits.get(s2, -1) < oid:
                waits[s2] = oid

        rkeys = [k for it in reads for k in _keys(it)]
        wkeys = [k for it in writes for k in _keys(it)]
        for k in rkeys:
            rec = self.res.get(k)
            if rec:
                need(rec["w"])
            if self.rr_edges and k.startswith("p") and eng in ("vector", "scalar"):
                other = "scalar" if eng == "vector" else "vector"
                lr = self.psum_reads.get(k)
                if lr and lr[0] == other:
                    need(lr[1])
        for k in wkeys:
            rec = self.res.get(k)
            if rec:
                need(rec["w"])
                for oid in rec["r"]:
                    need(oid)
        if dma:
            need(self.last_dma_on_sem.get(semname))
            self.last_dma_on_sem[semname] = opid
        if wait_all_of is not None:
            for s2, oid in self.last_dma_on_sem.items():
                if s2.startswith(wait_all_of):
                    need(oid)
        for s2, oid in waits.items():
            self.hwm[eng][s2] = max(self.hwm[eng].get(s2, -1), oid)
        self.eng_of.append(eng)
        self.sem_of.append(semname)
        self.ops.append({"eng": eng, "emit": emit, "waits": waits, "dma": dma})
        for k in rkeys:
            self.res.setdefault(k, {"w": None, "r": []})["r"].append(opid)
            if k.startswith("p") and eng in ("vector", "scalar"):
                self.psum_reads[k] = (eng, opid)
        for k in wkeys:
            self.res[k] = {"w": opid, "r": []}
        return opid

    def sem_names(self):
        names = list(ENGINES)
        for e in ("sync", "gpsimd"):
            names += [f"{e}_d{i}" for i in range(self.NDMASEM)]
        return names

    def emit(self, block, sems):
        referenced = set()
        for o in self.ops:
            referenced.update(o["waits"].values())
        count = {s: 0 for s in self.sem_names()}
        val = [0] * len(self.ops)
        incs = [0] * len(self.ops)
        for opid, o in enumerate(self.ops):
            s = self.sem_of[opid]
            inc = 16 if o["dma"] else (1 if opid in referenced else 0)
            incs[opid] = inc
            count[s] += inc
            val[opid] = count[s]
        streams = {e: [] for e in ENGINES}
        for opid, o in enumerate(self.ops):
            wv = {s2: val[oid] for s2, oid in o["waits"].items()}
            streams[o["eng"]].append((o["emit"], wv, incs[opid], self.sem_of[opid]))

        def run(eng_name, eng):
            for emit_fn, wv, inc, semname in streams[eng_name]:
                for s2 in sorted(wv):
                    eng.wait_ge(sems[s2], wv[s2])
                inst = emit_fn(eng)
                if inc:
                    inst.then_inc(sems[semname], inc)

        @block.sync
        def _(sync):
            run("sync", sync)

        @block.gpsimd
        def _(gpsimd):
            run("gpsimd", gpsimd)

        @block.tensor
        def _(tensor):
            run("tensor", tensor)

        @block.vector
        def _(vector):
            run("vector", vector)

        @block.scalar
        def _(scalar):
            run("scalar", scalar)


def build_program(phases=4, asub=5, npairs=NP, rsub=5, dbg=False, nexp=NE, tdbg=0):
    nc = bass.Bass("TRN2", target_bir_lowering=False)

    xT = nc.dram_tensor("xT", [D, L], F32, kind="ExternalInput").ap()
    wq_d = nc.dram_tensor("wq", [D, D], F32, kind="ExternalInput").ap()
    wvk_d = nc.dram_tensor("wvk", [D, 2 * D], F32, kind="ExternalInput").ap()
    wout_d = nc.dram_tensor("wout", [D, D], F32, kind="ExternalInput").ap()
    wrt_d = nc.dram_tensor("wrt", [D, 16], F32, kind="ExternalInput").ap()
    rbias_d = nc.dram_tensor("rbias", [1, 16], F32, kind="ExternalInput").ap()
    gnw_d = nc.dram_tensor("gnw", [128, 8], F32, kind="ExternalInput").ap()
    gnb_d = nc.dram_tensor("gnb", [128, 8], F32, kind="ExternalInput").ap()
    wg_d = nc.dram_tensor("wg", [NE, D, 512], F32, kind="ExternalInput").ap()
    wu_d = nc.dram_tensor("wu", [NE, D, 512], F32, kind="ExternalInput").ap()
    wd_d = nc.dram_tensor("wd", [NE, 512, D], F32, kind="ExternalInput").ap()
    dloc_d = nc.dram_tensor("dloc", [NP, 128, 256], F32, kind="ExternalInput").ap()
    qds_d = nc.dram_tensor("qds", [NP, 128, 128], F32, kind="ExternalInput").ap()
    kdv_d = nc.dram_tensor("kdv", [128, NH], F32, kind="ExternalInput").ap()
    g128_d = nc.dram_tensor("g128", [128, NP], F32, kind="ExternalInput").ap()
    ident_d = nc.dram_tensor("ident", [128, 128], F32, kind="ExternalInput").ap()
    out_d = nc.dram_tensor("out", [D, L], F32, kind="ExternalOutput").ap()
    if dbg:
        dbg_xn = nc.dram_tensor("dbg_xn", [D, L], F32, kind="ExternalOutput").ap()
        dbg_s = nc.dram_tensor("dbg_s", [D, L], F32, kind="ExternalOutput").ap()
        dbg_y = nc.dram_tensor("dbg_y", [D, L], F32, kind="ExternalOutput").ap()
        dbg_comb = nc.dram_tensor("dbg_comb", [128, 256], F32, kind="ExternalOutput").ap()
        dbg_cmb = nc.dram_tensor("dbg_cmb", [D, L], F32, kind="ExternalOutput").ap()
    lnrows_d = nc.dram_tensor("lnrows", [1, 2 * L], F32, kind="Internal").ap()
    combt_d = nc.dram_tensor("combt", [256, 128], F32, kind="Internal").ap()

    b = Builder()

    import contextlib
    with contextlib.ExitStack() as _es:
        AR = _es.enter_context(nc.sbuf_tensor("arena", [128, ARENA_W], F32))
        PS = _es.enter_context(nc.psum_tensor("parena", [128, 4096], F32))
        sems = {s: _es.enter_context(nc.semaphore(f"s_{s}")) for s in b.sem_names()}
        block = _es.enter_context(nc.Block())

        def a_f32(off, w):
            return AR[:, off:off + w]

        def a_bf(off, w):
            return AR[:, off:off + w].bitcast(BF16)

        XN = 0
        SBF = 8192
        P2 = 16384
        OV = 17024
        UP = OV + 27136

        xn_t = [a_bf(XN + k * 1024, 1024) for k in range(NDT)]
        s_sb = [a_bf(SBF + k * 1024, 1024) for k in range(NDT)]
        mrow = a_f32(SBF + 0, 2048)
        msqrow = a_f32(SBF + 2048, 2048)
        xsq = [a_f32(SBF + 4096 + i * 2048, 2048) for i in range(2)]
        m2row = a_f32(UP + 0, 2048)

        comb = a_f32(P2, 256)
        gnw = a_f32(P2 + 256, 8)
        gnb = a_f32(P2 + 264, 8)
        kdvec = a_f32(P2 + 272, 16)
        onesrow = a_f32(P2 + 288, 128)
        sones_f = a_f32(P2 + 416, 1)
        sones_b = a_bf(P2 + 417, 1)[:, 0:1]
        gones_f = a_f32(P2 + 418, 1)
        rbias_sb = a_f32(P2 + 420, 16)
        wrt_sb = [a_f32(P2 + 436 + k * 16, 16) for k in range(NDT)]
        GNS = P2 + 564
        st_pair = a_f32(GNS, 2)
        gn_row = a_f32(GNS + 2, 4)
        gn_row2 = a_f32(GNS + 6, 4)
        rstdv = a_f32(GNS + 10, 1)
        mrv = a_f32(GNS + 11, 1)
        scalev = a_f32(GNS + 12, 1)
        biasv = a_f32(GNS + 13, 1)
        gn_t1 = a_f32(GNS + 14, 1)

        x32 = [a_f32(OV + k * 2048, 2048) for k in range(NDT)]

        CMB = OV + 0
        WOUTS = OV + 8192
        O0, O1 = OV + 12288, OV + 14336
        QT, KT, VT, KDT = OV + 16384, OV + 17408, OV + 18432, OV + 19456
        WPQ, WPVK = OV + 20480, OV + 21504
        DLOC, QDS_ = OV + 23552, OV + 24576
        APBF = OV + 25088
        RINCS = OV + 27136
        RBFS = OV + 27648
        qt = a_bf(QT, 1024)
        kt = a_bf(KT, 1024)
        vt = a_bf(VT, 1024)
        kdt = a_bf(KDT, 1024)
        opair = [a_f32(O0, 2048), a_f32(O1, 2048)]
        apbf = a_bf(APBF, 2048)
        rincs = a_bf(RINCS, 512)
        rbfs = a_bf(RBFS, 512)
        cmb_t = [a_bf(CMB + k * 1024, 1024) for k in range(NDT)]
        wpq = [a_bf(WPQ + i * 512, 512) for i in range(2)]
        wpvk = [a_bf(WPVK + i * 1024, 1024) for i in range(2)]
        dloc = [a_bf(DLOC + hp * 128, 128) for hp in range(NP)]
        qds = [a_bf(QDS_ + hp * 64, 64) for hp in range(NP)]
        g128v = a_f32(P2 + 580, 8)
        wouts = [a_bf(WOUTS + k * 512, 512) for k in range(NDT)]

        YS = OV
        WST = OV + 16384
        SG = OV + 28672
        HB = OV + 30720
        CBB = OV + 34816
        FIN = WST
        y_all = a_f32(YS, 16384)
        y_t = [a_f32(YS + k * 2048, 2048) for k in range(NDT)]
        wgt = [a_bf(WST + i * 6144, 2048) for i in range(2)]
        wut = [a_bf(WST + i * 6144 + 2048, 2048) for i in range(2)]
        wdt = [a_bf(WST + i * 6144 + 4096, 2048) for i in range(2)]
        sgf = a_f32(SG, 2048)
        hb_t = [a_bf(HB + i * 1024, 1024) for i in range(4)]
        combB = a_bf(CBB, 1024)
        xfin = [a_f32(FIN + i * 2048, 2048) for i in range(2)]
        t1f = [a_f32(FIN + 4096 + i * 2048, 2048) for i in range(2)]
        ofin = [a_f32(FIN + 8192 + i * 2048, 2048) for i in range(2)]

        rstdB = a_f32(UP, 2048)
        mrstdB = a_f32(UP + 2048, 2048)
        ident = a_f32(UP + 4096, 128)
        raw = a_f32(UP + 4224, 256)
        mtok = a_f32(UP + 4480, 16)
        qtok = a_f32(UP + 4496, 16)
        vtok = a_f32(UP + 4512, 16)
        rstdt = a_f32(UP + 4528, 16)
        colsB = a_f32(UP + 4544, 16)
        biasB = a_f32(UP + 4560, 16)
        T2 = a_f32(UP + 4576, 64)
        GR = a_f32(UP + 4640, 64)
        M9 = a_f32(UP + 4704, 64)
        GM = a_f32(UP + 4768, 64)
        MK = a_f32(UP + 4864, 256)
        CE = a_f32(UP + 5120, 256)
        RK = a_f32(UP + 5376, 256)
        WW = a_f32(UP + 5632, 256)
        WS = a_f32(UP + 5888, 16)
        SCR = a_f32(UP + 5904, 512)
        combT = a_f32(UP + 6416, 256)
        qdfull = a_bf(UP + 6672, 1024)
        VTF = UP + 7696
        IDB = UP + 8720
        vtf = a_bf(VTF, 1024)
        identb = a_bf(IDB, 64)

        q = b.op
        rA = lambda off, n: A(off, n)

        q("sync", lambda e: e.dma_start(out=gnw, in_=gnw_d), writes=[A(P2 + 256, 8)], dma=True)
        q("sync", lambda e: e.dma_start(out=gnb, in_=gnb_d), writes=[A(P2 + 264, 8)], dma=True)
        q("sync", lambda e: e.dma_start(out=kdvec, in_=kdv_d), writes=[A(P2 + 272, 16)], dma=True)
        q("sync", lambda e: e.dma_start(out=rbias_sb[0:1, :], in_=rbias_d),
          writes=[A(P2 + 420, 16)], dma=True)
        q("sync", lambda e: e.dma_start(
            out=AR[:, P2 + 436:P2 + 436 + 128].rearrange("p (k c) -> p k c", c=16),
            in_=wrt_d.rearrange("(k p) c -> p k c", p=128)),
          writes=[A(P2 + 436, 128)], dma=True)
        q("sync", lambda e: e.dma_start(out=ident, in_=ident_d),
          writes=[A(UP + 4096, 128)], dma=True)
        q("gpsimd", lambda e: e.dma_start(out=identb, in_=ident_d),
          writes=[A(IDB, 64)], dma=True)
        q("vector", lambda e: e.memset(onesrow[0:1, :], 1.0), writes=[A(P2 + 288, 128)])
        q("vector", lambda e: e.memset(sones_f, 1.0 / 1024.0), writes=[A(P2 + 416, 1)])
        q("vector", lambda e: e.memset(sones_b, 1.0 / 1024.0), writes=[A(P2 + 417, 1)])
        q("vector", lambda e: e.memset(gones_f, 1.0 / 131072.0), writes=[A(P2 + 418, 1)])
        for k in range(NDT):
            q("sync", lambda e, k=k: e.dma_start(out=x32[k], in_=xT[k * 128:(k + 1) * 128, :]),
              writes=[A(OV + k * 2048, 2048)], dma=True)

        X3K = lambda k: A(OV + k * 2048, 2048)
        XSK = lambda d: A(SBF + 4096 + (d % 2) * 2048, 2048)
        for tc in range(4):
            for d in range(NDT):
                q("tensor", lambda e, tc=tc, d=d: e.matmul(
                    PS[0:1, tc * 512:(tc + 1) * 512], sones_f,
                    x32[d][:, tc * 512:(tc + 1) * 512], start=(d == 0), stop=(d == NDT - 1)),
                  reads=[X3K(d), A(P2 + 416, 1)], writes=[P(tc * 512, 512)])
        q("vector", lambda e: e.tensor_copy(mrow[0:1, :], PS[0:1, 0:2048]),
          reads=[P(0, 2048)], writes=[A(SBF, 2048)])
        for d in range(NDT):
            q("scalar", lambda e, d=d: e.activation(xsq[d % 2], x32[d], AF.Square),
              reads=[X3K(d)], writes=[XSK(d)])
            for tc in range(4):
                q("tensor", lambda e, tc=tc, d=d: e.matmul(
                    PS[0:1, 2048 + tc * 512:2048 + (tc + 1) * 512], sones_f,
                    xsq[d % 2][:, tc * 512:(tc + 1) * 512], start=(d == 0), stop=(d == NDT - 1)),
                  reads=[XSK(d), A(P2 + 416, 1)],
                  writes=[P(2048 + tc * 512, 512)])
        q("vector", lambda e: e.tensor_copy(msqrow[0:1, :], PS[0:1, 2048:4096]),
          reads=[P(2048, 2048)], writes=[A(SBF + 2048, 2048)])
        for d in range(NDT):
            for tc in range(4):
                q("tensor", lambda e, tc=tc, d=d: e.matmul(
                    PS[0:16, tc * 512:(tc + 1) * 512], wrt_sb[d],
                    x32[d][:, tc * 512:(tc + 1) * 512],
                    start=(d == 0), stop=(d == NDT - 1)),
                  reads=[X3K(d), A(P2 + 436, 128)], writes=[P(tc * 512, 512)])
        rawfm = AR[0:16, SBF + 6144:SBF + 8192]
        q("vector", lambda e: e.tensor_copy(rawfm, PS[0:16, 0:2048]),
          reads=[P(0, 2048)], writes=[A(SBF + 6144, 2048)])
        for c in range(NC):
            q("tensor", lambda e, c=c: e.transpose(
                PS[:, 1024 + c * 16:1024 + (c + 1) * 16],
                rawfm[:, c * 128:(c + 1) * 128], ident[0:16, 0:16]),
              reads=[A(SBF + 6144, 2048), A(UP + 4096, 128)], writes=[P(1024, 256)])
        q("vector", lambda e: e.tensor_copy(raw, PS[:, 1024:1280]),
          reads=[P(1024, 256)], writes=[A(UP + 4224, 256)])
        for d in range(NDT):
            q("tensor", lambda e, d=d: e.matmul(
                PS[0:1, 1536:1552], sones_f, wrt_sb[d], start=(d == 0), stop=(d == NDT - 1)),
              reads=[A(P2 + 436, 128), A(P2 + 416, 1)], writes=[P(1536, 16)])
        q("vector", lambda e: e.tensor_copy(SCR[0:1, 0:16], PS[0:1, 1536:1552]),
          reads=[P(1536, 16)], writes=[A(UP + 5904, 16)])
        q("tensor", lambda e: e.matmul(PS[:, 1552:1568], onesrow[0:1, :], SCR[0:1, 0:16],
                                       start=True, stop=True),
          reads=[A(UP + 5904, 16), A(P2 + 288, 128)], writes=[P(1536, 32)])
        q("vector", lambda e: e.tensor_scalar(out=colsB, in0=PS[:, 1552:1568], scalar1=1024.0,
                                              scalar2=None, op0=ALU.mult),
          reads=[P(1536, 32)], writes=[A(UP + 4544, 16)])
        q("tensor", lambda e: e.matmul(PS[:, 1568:1584], onesrow[0:1, :], rbias_sb[0:1, :],
                                       start=True, stop=True),
          reads=[A(P2 + 420, 16), A(P2 + 288, 128)], writes=[P(1536, 48)])
        q("vector", lambda e: e.tensor_copy(biasB, PS[:, 1568:1584]),
          reads=[P(1536, 48)], writes=[A(UP + 4560, 16)])

        q("vector", lambda e: e.tensor_tensor(m2row[0:1, :], mrow[0:1, :], mrow[0:1, :],
                                              op=ALU.mult),
          reads=[A(SBF, 2048)], writes=[A(UP, 2048)])
        q("vector", lambda e: e.tensor_tensor(msqrow[0:1, :], msqrow[0:1, :], m2row[0:1, :],
                                              op=ALU.subtract),
          reads=[A(SBF + 2048, 2048), A(UP, 2048)], writes=[A(SBF + 2048, 2048)])
        q("vector", lambda e: e.tensor_scalar(out=msqrow[0:1, :], in0=msqrow[0:1, :],
                                              scalar1=EPS, scalar2=None, op0=ALU.add),
          reads=[A(SBF + 2048, 2048)], writes=[A(SBF + 2048, 2048)])
        q("scalar", lambda e: e.activation(msqrow[0:1, :], msqrow[0:1, :], AF.Sqrt),
          reads=[A(SBF + 2048, 2048)], writes=[A(SBF + 2048, 2048)])
        q("vector", lambda e: e.reciprocal(msqrow[0:1, :], msqrow[0:1, :]),
          reads=[A(SBF + 2048, 2048)], writes=[A(SBF + 2048, 2048)])
        q("vector", lambda e: e.tensor_tensor(mrow[0:1, :], mrow[0:1, :], msqrow[0:1, :],
                                              op=ALU.mult),
          reads=[A(SBF, 2048), A(SBF + 2048, 2048)], writes=[A(SBF, 2048)])
        q("sync", lambda e: e.dma_start(out=lnrows_d[0:1, 0:L], in_=msqrow[0:1, :]),
          reads=[A(SBF + 2048, 2048)], writes=["lnra"], dma=True)
        q("sync", lambda e: e.dma_start(out=lnrows_d[0:1, L:2 * L], in_=mrow[0:1, :]),
          reads=[A(SBF, 2048)], writes=["lnrb"], dma=True)
        for half in range(2):
            bsrc = msqrow if half == 0 else mrow
            skey = A(SBF + 2048, 2048) if half == 0 else A(SBF, 2048)
            dst = rstdB if half == 0 else mrstdB
            dkey = A(UP, 2048) if half == 0 else A(UP + 2048, 2048)
            pb = 2048 if half == 0 else 0
            for tc in range(4):
                q("tensor", lambda e, bsrc=bsrc, tc=tc, pb=pb: e.matmul(
                    PS[:, pb + tc * 512:pb + (tc + 1) * 512], onesrow[0:1, :],
                    bsrc[0:1, tc * 512:(tc + 1) * 512], start=True, stop=True),
                  reads=[skey, A(P2 + 288, 128)], writes=[P(pb + tc * 512, 512)])
            for tc in range(4):
                q("scalar", lambda e, dst=dst, tc=tc, pb=pb: e.copy(
                    dst[:, tc * 512:(tc + 1) * 512], PS[:, pb + tc * 512:pb + (tc + 1) * 512]),
                  reads=[P(pb + tc * 512, 512)], writes=[dkey])
        sc16a = AR[0:16, UP + 5904 + 128:UP + 5904 + 256]
        sc16b = AR[0:16, UP + 5904 + 256:UP + 5904 + 384]
        q("sync", lambda e: e.dma_start(
            out=sc16a, in_=lnrows_d[0, 0:L].rearrange("(c p) -> c p", p=128)),
          reads=["lnra"], writes=[A(UP + 6032, 128)], dma=True)
        q("sync", lambda e: e.dma_start(
            out=sc16b, in_=lnrows_d[0, L:2 * L].rearrange("(c p) -> c p", p=128)),
          reads=["lnrb"], writes=[A(UP + 6160, 128)], dma=True)
        q("tensor", lambda e: e.transpose(PS[:, 3584:3600], sc16a, ident[0:16, 0:16]),
          reads=[A(UP + 6032, 128), A(UP + 4096, 128)], writes=[P(3584, 16)])
        q("tensor", lambda e: e.transpose(PS[:, 3600:3616], sc16b, ident[0:16, 0:16]),
          reads=[A(UP + 6160, 128), A(UP + 4096, 128)], writes=[P(3584, 32)])
        q("scalar", lambda e: e.copy(rstdt, PS[:, 3584:3600]),
          reads=[P(3584, 16)], writes=[A(UP + 4528, 16)])
        q("scalar", lambda e: e.copy(mtok, PS[:, 3600:3616]),
          reads=[P(3584, 32)], writes=[A(UP + 4480, 16)])

        for d in range(NDT):
            eng = "vector" if d < 6 else "gpsimd"
            q(eng, lambda e, d=d: e.tensor_tensor(x32[d], x32[d], rstdB, op=ALU.mult),
              reads=[X3K(d), A(UP, 2048)], writes=[X3K(d)])
            q(eng, lambda e, d=d: e.tensor_tensor(xn_t[d], x32[d], mrstdB, op=ALU.subtract),
              reads=[X3K(d), A(UP + 2048, 2048)], writes=[A(XN + d * 1024, 1024)])

        routing_thunks = []

        def qr(eng, emit, reads=(), writes=()):
            routing_thunks.append(lambda: q(eng, emit, reads=reads, writes=writes))

        def drain_routing(k):
            for _ in range(k):
                if routing_thunks:
                    routing_thunks.pop(0)()

        def emit_routing():
            raw3 = raw.rearrange("p (c e) -> p c e", e=16)
            qr("vector", lambda e: e.tensor_tensor(
                CE, mtok[:, :, None].broadcast_to([128, 16, 16]),
                colsB[:, None, :].broadcast_to([128, 16, 16]), op=ALU.mult),
                reads=[A(UP + 4480, 16), A(UP + 4544, 16)], writes=[A(UP + 5120, 256)])
            qr("vector", lambda e: e.tensor_tensor(
                raw3, raw3, rstdt[:, :, None].broadcast_to([128, 16, 16]), op=ALU.mult),
                reads=[A(UP + 4224, 256), A(UP + 4528, 16)], writes=[A(UP + 4224, 256)])
            qr("vector", lambda e: e.tensor_tensor(raw, raw, CE, op=ALU.subtract),
                reads=[A(UP + 4224, 256), A(UP + 5120, 256)], writes=[A(UP + 4224, 256)])
            qr("scalar", lambda e: e.activation(raw, raw, AF.Sigmoid),
                reads=[A(UP + 4224, 256)], writes=[A(UP + 4224, 256)])
            qr("vector", lambda e: e.tensor_tensor(
                MK.rearrange("p (c e) -> p c e", e=16), raw3,
                biasB[:, None, :].broadcast_to([128, 16, 16]), op=ALU.add),
                reads=[A(UP + 4224, 256), A(UP + 4560, 16)], writes=[A(UP + 4864, 256)])
            pair_list = [(0, 1), (0, 2), (0, 3), (1, 2), (1, 3), (2, 3)]
            for g in range(4):
                for pi, (i, j) in enumerate(pair_list):
                    ei, ej = 4 * g + i, 4 * g + j
                    qr("vector", lambda e, ei=ei, ej=ej: e.tensor_tensor(
                        CE[:, 0:256:16], MK[:, ei:256:16], MK[:, ej:256:16], op=ALU.add),
                        reads=[A(UP + 4864, 256)], writes=[A(UP + 5120, 256)])
                    if pi == 0:
                        qr("vector", lambda e, g=g: e.tensor_copy(T2[:, g:64:4], CE[:, 0:256:16]),
                            reads=[A(UP + 5120, 256)], writes=[A(UP + 4576, 64)])
                    else:
                        qr("vector", lambda e, g=g: e.tensor_tensor(
                            T2[:, g:64:4], T2[:, g:64:4], CE[:, 0:256:16], op=ALU.max),
                            reads=[A(UP + 5120, 256), A(UP + 4576, 64)], writes=[A(UP + 4576, 64)])
            for g in range(4):
                others = [gg for gg in range(4) if gg != g]
                for oi, gg in enumerate(others):
                    qr("vector", lambda e, g=g, gg=gg: e.tensor_tensor(
                        CE[:, 0:64:4], T2[:, gg:64:4], T2[:, g:64:4], op=ALU.is_gt),
                        reads=[A(UP + 4576, 64)], writes=[A(UP + 5120, 64)])
                    if oi == 0:
                        qr("vector", lambda e, g=g: e.tensor_copy(GR[:, g:64:4], CE[:, 0:64:4]),
                            reads=[A(UP + 5120, 64)], writes=[A(UP + 4640, 64)])
                    else:
                        qr("vector", lambda e, g=g: e.tensor_tensor(
                            GR[:, g:64:4], GR[:, g:64:4], CE[:, 0:64:4], op=ALU.add),
                            reads=[A(UP + 5120, 64), A(UP + 4640, 64)], writes=[A(UP + 4640, 64)])
            qr("vector", lambda e: e.tensor_scalar(out=GM, in0=GR, scalar1=1.5, scalar2=None,
                                                   op0=ALU.is_lt),
                reads=[A(UP + 4640, 64)], writes=[A(UP + 4768, 64)])
            qr("vector", lambda e: e.tensor_scalar(out=M9, in0=GM, scalar1=1.0, scalar2=1e9,
                                                   op0=ALU.subtract, op1=ALU.mult),
                reads=[A(UP + 4768, 64)], writes=[A(UP + 4704, 64)])
            for e_ in range(16):
                g = e_ // 4
                qr("vector", lambda e, e_=e_, g=g: e.tensor_tensor(
                    MK[:, e_:256:16], MK[:, e_:256:16], GM[:, g:64:4], op=ALU.mult),
                    reads=[A(UP + 4864, 256), A(UP + 4768, 64)], writes=[A(UP + 4864, 256)])
                qr("vector", lambda e, e_=e_, g=g: e.tensor_tensor(
                    MK[:, e_:256:16], MK[:, e_:256:16], M9[:, g:64:4], op=ALU.add),
                    reads=[A(UP + 4864, 256), A(UP + 4704, 64)], writes=[A(UP + 4864, 256)])
            for e_ in range(16):
                qr("vector", lambda e, e_=e_: e.tensor_tensor(
                    CE.rearrange("p (c e) -> p c e", e=16),
                    MK.rearrange("p (c e) -> p c e", e=16),
                    MK[:, e_:256:16][:, :, None].broadcast_to([128, 16, 16]), op=ALU.is_gt),
                    reads=[A(UP + 4864, 256)], writes=[A(UP + 5120, 256)])
                qr("vector", lambda e, e_=e_: e.reduce_sum(
                    RK[:, e_:256:16], CE.rearrange("p (c e) -> p c e", e=16), axis=AX.X),
                    reads=[A(UP + 5120, 256)], writes=[A(UP + 5376, 256)])
            qr("vector", lambda e: e.tensor_scalar(out=RK, in0=RK, scalar1=3.5, scalar2=None,
                                                   op0=ALU.is_lt),
                reads=[A(UP + 5376, 256)], writes=[A(UP + 5376, 256)])
            qr("vector", lambda e: e.tensor_tensor(WW, raw, RK, op=ALU.mult),
                reads=[A(UP + 4224, 256), A(UP + 5376, 256)], writes=[A(UP + 5632, 256)])
            qr("vector", lambda e: e.reduce_sum(WS, WW.rearrange("p (c e) -> p c e", e=16),
                                                axis=AX.X),
                reads=[A(UP + 5632, 256)], writes=[A(UP + 5888, 16)])
            qr("vector", lambda e: e.tensor_scalar(out=WS, in0=WS, scalar1=1e-20, scalar2=None,
                                                   op0=ALU.add),
                reads=[A(UP + 5888, 16)], writes=[A(UP + 5888, 16)])
            qr("vector", lambda e: e.reciprocal(WS, WS),
                reads=[A(UP + 5888, 16)], writes=[A(UP + 5888, 16)])
            qr("vector", lambda e: e.tensor_scalar(out=WS, in0=WS, scalar1=2.5, scalar2=None,
                                                   op0=ALU.mult),
                reads=[A(UP + 5888, 16)], writes=[A(UP + 5888, 16)])
            qr("vector", lambda e: e.tensor_tensor(
                comb.rearrange("p (c e) -> p c e", e=16), WW.rearrange("p (c e) -> p c e", e=16),
                WS[:, :, None].broadcast_to([128, 16, 16]), op=ALU.mult),
                reads=[A(UP + 5632, 256), A(UP + 5888, 16)], writes=[A(P2, 256)])

        def emit_routing_tail():
            drain_routing(len(routing_thunks))
            q("tensor", lambda e: e.transpose(PS[:, 3584:3712], comb[:, 0:128], ident),
              reads=[A(P2, 256), A(UP + 4096, 128)], writes=[P(3584, 128)])
            q("scalar", lambda e: e.copy(combT[:, 0:128], PS[:, 3584:3712]),
              reads=[P(3584, 128)], writes=[A(UP + 6416, 128)])
            q("tensor", lambda e: e.transpose(PS[:, 3584:3712], comb[:, 128:256], ident),
              reads=[A(P2, 256), A(UP + 4096, 128)], writes=[P(3584, 128)])
            q("scalar", lambda e: e.copy(combT[:, 128:256], PS[:, 3584:3712]),
              reads=[P(3584, 128)], writes=[A(UP + 6544, 128)])
            q("sync", lambda e: e.dma_start(out=combt_d[0:128, :], in_=combT[:, 0:128]),
              reads=[A(UP + 6416, 128)], writes=["combt"], dma=True)
            q("sync", lambda e: e.dma_start(out=combt_d[128:256, :], in_=combT[:, 128:256]),
              reads=[A(UP + 6544, 128)], writes=["combt"], dma=True)


        XNK = lambda d: A(XN + d * 1024, 1024)
        if phases < 2:
            q("sync", lambda e: e.nop(), wait_all_of="sync")
            b.emit(block, sems)
            return nc

        def load_pair(hp):
            buf = hp % 2
            q("gpsimd", lambda e, hp=hp, buf=buf: e.dma_start(
                out=wpq[buf].rearrange("p (k c) -> p k c", c=128),
                in_=wq_d[:, hp * 128:(hp + 1) * 128].rearrange("(k p) c -> p k c", p=128)),
              writes=[A(WPQ + buf * 512, 512)], dma=True)
            q("gpsimd", lambda e, hp=hp, buf=buf: e.dma_start(
                out=wpvk[buf].rearrange("p (k c) -> p k c", c=256),
                in_=wvk_d[:, hp * 256:(hp + 1) * 256].rearrange("(k p) c -> p k c", p=128)),
              writes=[A(WPVK + buf * 1024, 1024)], dma=True)

        load_pair(0)
        load_pair(1)
        q("sync", lambda e: e.dma_start(out=g128v, in_=g128_d),
          writes=[A(P2 + 580, 8)], dma=True)
        for hp in range(NP):
            q("gpsimd", lambda e, hp=hp: e.dma_start(out=dloc[hp], in_=dloc_d[hp]),
              writes=[A(DLOC + hp * 128, 128)], dma=True)
            q("gpsimd", lambda e, hp=hp: e.dma_start(out=qds[hp], in_=qds_d[hp]),
              writes=[A(QDS_ + hp * 64, 64)], dma=True)
        q("gpsimd", lambda e: e.dma_start(
            out=AR[:, WOUTS:WOUTS + 4096].bitcast(BF16).rearrange("p (k c) -> p k c", c=1024),
            in_=wout_d.rearrange("(k p) c -> p k c", p=128)),
          writes=[A(WOUTS, 4096)], dma=True)

        def emit_gn_stats(hp):
            ob = hp % 2
            OKEY = A(O0 if ob == 0 else O1, 2048)
            q("vector", lambda e, ob=ob: e.reduce_sum(st_pair[:, 0:1], opair[ob], axis=AX.X),
              reads=[OKEY], writes=[A(GNS, 2)])
            q("scalar", lambda e, ob=ob: e.activation(
                AR[:, SBF:SBF + 2048], opair[ob], AF.Square),
              reads=[OKEY], writes=[A(SBF, 2048)])
            q("vector", lambda e: e.reduce_sum(st_pair[:, 1:2], AR[:, SBF:SBF + 2048],
                                               axis=AX.X),
              reads=[A(SBF, 2048)], writes=[A(GNS, 2)])

        def emit_gn_mm1(hp):
            for h01 in range(2):
                o64 = h01 * 64
                base = 3072 if h01 == 0 else 3584
                for s01 in range(2):
                    q("tensor", lambda e, o64=o64, s01=s01, base=base: e.matmul(
                        PS[0:1, base + s01:base + s01 + 1],
                        st_pair[o64:o64 + 64, s01:s01 + 1], gones_f[o64:o64 + 64, :],
                        start=True, stop=True),
                      reads=[A(GNS, 2), A(P2 + 418, 1)], writes=[P(base, 4)])

        def emit_gn(hp):
            ob = hp % 2
            OKEY = A(O0 if ob == 0 else O1, 2048)
            q("scalar", lambda e: e.copy(gn_row[0:1, 0:2], PS[0:1, 3072:3074]),
              reads=[P(3072, 4)], writes=[A(GNS + 2, 4)])
            q("scalar", lambda e: e.copy(gn_row[0:1, 2:4], PS[0:1, 3584:3586]),
              reads=[P(3584, 4)], writes=[A(GNS + 2, 4)])
            q("vector", lambda e: e.tensor_tensor(gn_row2[0:1, 0:4:2], gn_row[0:1, 0:4:2],
                                                  gn_row[0:1, 0:4:2], op=ALU.mult),
              reads=[A(GNS + 2, 4)], writes=[A(GNS + 6, 4)])
            q("vector", lambda e: e.tensor_tensor(gn_row2[0:1, 0:4:2], gn_row[0:1, 1:4:2],
                                                  gn_row2[0:1, 0:4:2], op=ALU.subtract),
              reads=[A(GNS + 2, 4), A(GNS + 6, 4)], writes=[A(GNS + 6, 4)])
            q("vector", lambda e: e.tensor_scalar(out=gn_row2[0:1, 0:4:2],
                                                  in0=gn_row2[0:1, 0:4:2], scalar1=EPS,
                                                  scalar2=None, op0=ALU.add),
              reads=[A(GNS + 6, 4)], writes=[A(GNS + 6, 4)])
            q("scalar", lambda e: e.activation(gn_row2[0:1, 0:4:2], gn_row2[0:1, 0:4:2],
                                               AF.Ln),
              reads=[A(GNS + 6, 4)], writes=[A(GNS + 6, 4)])
            q("vector", lambda e: e.tensor_scalar(out=gn_row2[0:1, 0:4:2],
                                                  in0=gn_row2[0:1, 0:4:2], scalar1=-0.5,
                                                  scalar2=None, op0=ALU.mult),
              reads=[A(GNS + 6, 4)], writes=[A(GNS + 6, 4)])
            q("scalar", lambda e: e.activation(gn_row2[0:1, 0:4:2], gn_row2[0:1, 0:4:2],
                                               AF.Exp),
              reads=[A(GNS + 6, 4)], writes=[A(GNS + 6, 4)])
            q("vector", lambda e: e.tensor_tensor(gn_row2[0:1, 1:4:2], gn_row[0:1, 0:4:2],
                                                  gn_row2[0:1, 0:4:2], op=ALU.mult),
              reads=[A(GNS + 2, 4), A(GNS + 6, 4)], writes=[A(GNS + 6, 4)])
            for h01 in range(2):
                o64 = h01 * 64
                q("tensor", lambda e, h01=h01, o64=o64: e.matmul(
                    PS[o64:o64 + 64, 3076:3077], onesrow[0:1, 0:64],
                    gn_row2[0:1, 2 * h01:2 * h01 + 1], start=True, stop=True),
                  reads=[A(GNS + 6, 4), A(P2 + 288, 128)], writes=[P(3076, 2)])
                q("tensor", lambda e, h01=h01, o64=o64: e.matmul(
                    PS[o64:o64 + 64, 3077:3078], onesrow[0:1, 0:64],
                    gn_row2[0:1, 2 * h01 + 1:2 * h01 + 2], start=True, stop=True),
                  reads=[A(GNS + 6, 4), A(P2 + 288, 128)], writes=[P(3076, 2)])
            q("scalar", lambda e: e.copy(rstdv, PS[:, 3076:3077]),
              reads=[P(3076, 2)], writes=[A(GNS + 10, 1)])
            q("scalar", lambda e: e.copy(mrv, PS[:, 3077:3078]),
              reads=[P(3076, 2)], writes=[A(GNS + 11, 1)])
            q("vector", lambda e, hp=hp: e.tensor_tensor(scalev, rstdv, gnw[:, hp:hp + 1],
                                                         op=ALU.mult),
              reads=[A(GNS + 10, 1), A(P2 + 256, 8)], writes=[A(GNS + 12, 1)])
            q("vector", lambda e, hp=hp: e.tensor_tensor(gn_t1, mrv, gnw[:, hp:hp + 1],
                                                         op=ALU.mult),
              reads=[A(GNS + 11, 1), A(P2 + 256, 8)], writes=[A(GNS + 14, 1)])
            q("vector", lambda e, hp=hp: e.tensor_tensor(biasv, gnb[:, hp:hp + 1], gn_t1,
                                                         op=ALU.subtract),
              reads=[A(GNS + 14, 1), A(P2 + 264, 8)], writes=[A(GNS + 13, 1)])
            q("vector", lambda e, hp=hp, ob=ob: e.scalar_tensor_tensor(
                out=cmb_t[hp], in0=opair[ob], scalar=scalev,
                in1=biasv.broadcast_to([128, 2048]), op0=ALU.mult, op1=ALU.add),
              reads=[OKEY, A(GNS + 12, 1), A(GNS + 13, 1)],
              writes=[A(CMB + hp * 1024, 1024)])


        for hp in range(npairs):
            buf = hp % 2
            if asub < 1:
                continue
            if hp >= 1 and asub >= 4:
                emit_gn_stats(hp - 1)
            KWQ = A(WPQ + buf * 512, 512)
            KWVK = A(WPVK + buf * 1024, 1024)
            for which in range(3):
                dest = (qt, kt, vtf)[which]
                dkey = (A(QT, 1024), A(KT, 1024), A(VTF, 1024))[which]
                base = (0, 2048, 0)[which]
                for d in range(NDT):
                    lh = (wpq[buf][:, d * 128:(d + 1) * 128] if which == 0
                          else wpvk[buf][:, d * 256 + 128:(d + 1) * 256] if which == 1
                          else wpvk[buf][:, d * 256:d * 256 + 128])
                    lk = KWQ if which == 0 else KWVK
                    for tc in range(4):
                        q("tensor", lambda e, lh=lh, tc=tc, d=d, base=base: e.matmul(
                            PS[:, base + tc * 512:base + (tc + 1) * 512], lh,
                            xn_t[d][:, tc * 512:(tc + 1) * 512],
                            start=(d == 0), stop=(d == NDT - 1)),
                          reads=[lk, XNK(d)], writes=[P(base + tc * 512, 512)])
                for tc in range(4):
                    q("scalar", lambda e, dest=dest, tc=tc, base=base: e.copy(
                        dest[:, tc * 512:(tc + 1) * 512], PS[:, base + tc * 512:base + (tc + 1) * 512]),
                      reads=[P(base + tc * 512, 512)], writes=[dkey])
            PSB = PS.bitcast(BF16)
            for c in range(NC if asub >= 2 else 0):
                pw = 1024 + (c % 2) * 512
                if not (tdbg & 2):
                    q("tensor", lambda e, c=c, pw=pw: e.transpose(
                        PSB[:, 2 * pw:2 * pw + 128], vtf[:, c * 128:(c + 1) * 128], identb),
                      reads=[A(VTF, 1024), A(IDB, 64)], writes=[P(pw, 64)])
                    q("scalar", lambda e, c=c, pw=pw: e.copy(
                        vt[:, c * 128:(c + 1) * 128], PSB[:, 2 * pw:2 * pw + 128]),
                      reads=[P(pw, 128)], writes=[A(VT, 1024)])
                if not (tdbg & 1):
                    pk = [(c % 2) * 512, 3072 + (c % 2) * 512]
                    for h01 in range(2):
                        q("tensor", lambda e, c=c, h01=h01, pko=pk[h01]: e.transpose(
                            PSB[:, 2 * pko:2 * pko + 64],
                            kt[h01 * 64:h01 * 64 + 64, c * 128:(c + 1) * 128],
                            identb[h01 * 64:h01 * 64 + 64, h01 * 64:h01 * 64 + 64]),
                          reads=[A(KT, 1024), A(IDB, 64)], writes=[P(pk[h01], 32)])
                    for h01 in range(2):
                        h = 2 * hp + h01
                        q("scalar", lambda e, c=c, h01=h01, h=h, pko=pk[h01]: e.activation(
                            kdt[:, c * 128 + h01 * 64:c * 128 + (h01 + 1) * 64],
                            PSB[:, 2 * pko:2 * pko + 64],
                            AF.Copy, scale=kdvec[:, h:h + 1]),
                          reads=[P(pk[h01], 32), A(P2 + 272, 16)], writes=[A(KDT, 1024)])
            if hp + 2 < npairs:
                load_pair(hp + 2)
            if hp == 0:
                emit_routing()
            else:
                if hp == 2 or (hp == npairs - 1 and npairs <= 2):
                    emit_routing_tail()
                if asub >= 4:
                    emit_gn_mm1(hp - 1)
            if asub < 3:
                continue
            q("vector", lambda e, hp=hp: e.tensor_tensor(
                qdfull.rearrange("p (c a) -> p c a", a=128),
                qt.rearrange("p (c a) -> p c a", a=128),
                qds[hp][:, None, :].broadcast_to([128, 16, 128]), op=ALU.mult),
              reads=[A(QT, 1024), A(QDS_ + hp * 64, 64)], writes=[A(UP + 6672, 1024)])
            if rsub < 2:
                continue
            q("vector", lambda e: e.memset(rbfs[:, 0:64], 0.0),
              writes=[A(RBFS, 32)])
            for c in range(NC):
                pa0 = 2048 + (c % 2) * 512
                pa1 = 1024 + (c % 2) * 512
                pr = (c % 2) * 512
                for h01 in range(2):
                    o64 = h01 * 64
                    pa = pa0 if h01 == 0 else pa1
                    q("tensor", lambda e, c=c, h01=h01, o64=o64, pa=pa: e.matmul(
                        PS[:, pa:pa + 128],
                        kt[o64:o64 + 64, c * 128:(c + 1) * 128],
                        qt[o64:o64 + 64, c * 128:(c + 1) * 128], start=True, stop=True,
                        tile_position=(o64, 0)),
                      reads=[A(KT, 1024), A(QT, 1024)], writes=[P(pa, 128)])
                    q("vector", lambda e, c=c, hp=hp, pa=pa, h01=h01: e.tensor_tensor(
                        apbf[:, c * 256 + h01 * 128:c * 256 + (h01 + 1) * 128],
                        PS[:, pa:pa + 128],
                        dloc[hp][:, h01 * 128:(h01 + 1) * 128], op=ALU.mult),
                      reads=[P(pa, 128), A(DLOC + hp * 128, 128)],
                      writes=[A(APBF + c * 128, 128)])
                if rsub >= 4:
                    q("tensor", lambda e, c=c, pr=pr: e.matmul(
                        PS[:, pr + 256:pr + 384],
                        kdt[:, c * 128:(c + 1) * 128],
                        vt[:, c * 128:(c + 1) * 128], start=True, stop=True),
                      reads=[A(KDT, 1024), A(VT, 1024)], writes=[P(pr + 256, 128)])
                    for h01 in range(2):
                        o64 = h01 * 64
                        q("scalar", lambda e, c=c, pr=pr, o64=o64: e.copy(
                            rincs[o64:o64 + 64, c * 64:(c + 1) * 64],
                            PS[o64:o64 + 64, pr + 256 + o64:pr + 320 + o64]),
                          reads=[P(pr + 256, 128)], writes=[A(RINCS + c * 32, 32)])
                    if c < NC - 1:
                        q("vector", lambda e, c=c, hp=hp: e.scalar_tensor_tensor(
                            out=rbfs[:, (c + 1) * 64:(c + 2) * 64],
                            in0=rbfs[:, c * 64:(c + 1) * 64],
                            scalar=g128v[:, hp:hp + 1],
                            in1=rincs[:, c * 64:(c + 1) * 64],
                            op0=ALU.mult, op1=ALU.add),
                          reads=[A(RBFS + c * 32, 32), A(RINCS + c * 32, 32),
                                 A(P2 + 580, 8)],
                          writes=[A(RBFS + (c + 1) * 32, 32)])
                drain_routing(5)
            if hp >= 1 and asub >= 4:
                emit_gn(hp - 1)
            ob = hp % 2
            for c in range(NC):
                po = 3072 + (c % 2) * 512
                for h01 in range(2):
                    o64 = h01 * 64
                    q("tensor", lambda e, c=c, h01=h01, o64=o64, po=po: e.matmul(
                        PS[o64:o64 + 64, po:po + 128],
                        vt[:, c * 128 + o64:c * 128 + o64 + 64],
                        apbf[:, c * 256 + h01 * 128:c * 256 + (h01 + 1) * 128],
                        start=True, stop=(c == 0 or rsub < 5)),
                      reads=[A(VT, 1024), A(APBF + c * 128, 128)],
                      writes=[P(po, 128)])
                    if c > 0 and rsub >= 5:
                        q("tensor", lambda e, c=c, o64=o64, po=po: e.matmul(
                            PS[o64:o64 + 64, po:po + 128],
                            rbfs[o64:o64 + 64, c * 64:(c + 1) * 64],
                            qdfull[o64:o64 + 64, c * 128:(c + 1) * 128],
                            start=False, stop=True),
                          reads=[A(RBFS + c * 32, 32), A(UP + 6672, 1024)],
                          writes=[P(po, 128)])
                q("scalar", lambda e, c=c, po=po, ob=ob: e.copy(
                    opair[ob][:, c * 128:(c + 1) * 128], PS[:, po:po + 128]),
                  reads=[P(po, 128)], writes=[A(O0 if ob == 0 else O1, 2048)])
                drain_routing(3)

        if asub >= 4 and npairs > 0:
            emit_gn_stats(npairs - 1)
            emit_gn_mm1(npairs - 1)
            emit_gn(npairs - 1)

        for ot in range(NDT if asub >= 5 else 0):
            po = (ot % 2) * 2048
            for d in range(NDT):
                for tc in range(4):
                    q("tensor", lambda e, ot=ot, tc=tc, d=d, po=po: e.matmul(
                        PS[:, po + tc * 512:po + (tc + 1) * 512],
                        wouts[d][:, ot * 128:(ot + 1) * 128],
                        cmb_t[d][:, tc * 512:(tc + 1) * 512],
                        start=(d == 0), stop=(d == NDT - 1)),
                      reads=[A(WOUTS + d * 512, 512), A(CMB + d * 1024, 1024)],
                      writes=[P(po + tc * 512, 512)])
            for tc in range(4):
                q("scalar", lambda e, ot=ot, tc=tc, po=po: e.copy(
                    s_sb[ot][:, tc * 512:(tc + 1) * 512], PS[:, po + tc * 512:po + (tc + 1) * 512]),
                  reads=[P(po + tc * 512, 512)], writes=[A(SBF + ot * 1024, 1024)])

        if phases < 3:
            q("sync", lambda e: e.nop(), wait_all_of="sync")
            b.emit(block, sems)
            return nc

        def load_expert(ex):
            buf = ex % 2
            q("gpsimd", lambda e, ex=ex, buf=buf: e.dma_start(
                out=wgt[buf].rearrange("p (k c) -> p k c", c=512),
                in_=wg_d[ex].rearrange("(k p) c -> p k c", p=128)),
              writes=[A(WST + buf * 6144, 2048)], dma=True)
            q("gpsimd", lambda e, ex=ex, buf=buf: e.dma_start(
                out=wut[buf].rearrange("p (k c) -> p k c", c=512),
                in_=wu_d[ex].rearrange("(k p) c -> p k c", p=128)),
              writes=[A(WST + buf * 6144 + 2048, 2048)], dma=True)
            q("gpsimd", lambda e, ex=ex, buf=buf: e.dma_start(
                out=wdt[buf].rearrange("p (k c) -> p k c", c=1024),
                in_=wd_d[ex].rearrange("(k p) c -> p k c", p=128)),
              writes=[A(WST + buf * 6144 + 4096, 2048)], dma=True)

        def load_combB(ex):
            q("gpsimd", lambda e, ex=ex: e.dma_start(
                out=combB.rearrange("p (c t) -> p c t", t=128),
                in_=combt_d[ex:256:16, :].unsqueeze(0).broadcast_to([128, 16, 128])),
              reads=["combt"], writes=[A(CBB, 1024)], dma=True)

        b.rr_edges = False
        load_expert(0)
        load_combB(0)

        for ex in range(nexp):
            buf = ex % 2
            KWG = A(WST + buf * 6144, 2048)
            KWU = A(WST + buf * 6144 + 2048, 2048)
            KWD = A(WST + buf * 6144 + 4096, 2048)
            if ex + 1 < nexp:
                load_expert(ex + 1)
            for it in range(4):
                for d in range(NDT):
                    for tc in range(4):
                        q("tensor", lambda e, it=it, d=d, tc=tc, buf=buf: e.matmul(
                            PS[:, tc * 512:(tc + 1) * 512],
                            wgt[buf][:, d * 512 + it * 128:d * 512 + (it + 1) * 128],
                            xn_t[d][:, tc * 512:(tc + 1) * 512],
                            start=(d == 0), stop=(d == NDT - 1)),
                          reads=[KWG, XNK(d)], writes=[P(tc * 512, 512)])
                q("scalar", lambda e: e.activation(sgf, PS[:, 0:2048], AF.Silu),
                  reads=[P(0, 2048)], writes=[A(SG, 2048)])
                for d in range(NDT):
                    for tc in range(4):
                        q("tensor", lambda e, it=it, d=d, tc=tc, buf=buf: e.matmul(
                            PS[:, 2048 + tc * 512:2048 + (tc + 1) * 512],
                            wut[buf][:, d * 512 + it * 128:d * 512 + (it + 1) * 128],
                            xn_t[d][:, tc * 512:(tc + 1) * 512],
                            start=(d == 0), stop=(d == NDT - 1)),
                          reads=[KWU, XNK(d)], writes=[P(2048 + tc * 512, 512)])
                for tc in range(4):
                    q("vector", lambda e, tc=tc: e.tensor_tensor(
                        sgf[:, tc * 512:(tc + 1) * 512], sgf[:, tc * 512:(tc + 1) * 512],
                        combB[:, c2 * 512:(c2 + 1) * 512] if False else combB[:, tc * 512:(tc + 1) * 512],
                        op=ALU.mult),
                      reads=[A(SG + tc * 512, 512), A(CBB + tc * 256, 256)],
                      writes=[A(SG + tc * 512, 512)])
                for tc in range(4):
                    q("vector", lambda e, it=it, tc=tc: e.tensor_tensor(
                        hb_t[it][:, tc * 512:(tc + 1) * 512],
                        sgf[:, tc * 512:(tc + 1) * 512],
                        PS[:, 2048 + tc * 512:2048 + (tc + 1) * 512], op=ALU.mult),
                      reads=[P(2048 + tc * 512, 512), A(SG + tc * 512, 512)],
                      writes=[A(HB + it * 1024 + tc * 256, 256)])
            if ex + 1 < nexp:
                load_combB(ex + 1)
            last = (ex == nexp - 1) and phases >= 4
            if last:
                for k in range(2):
                    q("gpsimd", lambda e, k=k: e.dma_start(
                        out=xfin[k], in_=xT[k * 128:(k + 1) * 128, :]),
                      writes=[A(FIN + k * 2048, 2048)], dma=True)
            for ot in range(NDT):
                po = (ot % 2) * 2048
                for it in range(4):
                    for tc in range(4):
                        q("tensor", lambda e, ot=ot, it=it, tc=tc, buf=buf, po=po: e.matmul(
                            PS[:, po + tc * 512:po + (tc + 1) * 512],
                            wdt[buf][:, it * 1024 + ot * 128:it * 1024 + (ot + 1) * 128],
                            hb_t[it][:, tc * 512:(tc + 1) * 512],
                            start=(it == 0), stop=(it == 3)),
                          reads=[KWD, A(HB + it * 1024 + tc * 256, 256)],
                          writes=[P(po + tc * 512, 512)])
                if ex == 0:
                    q("vector", lambda e, ot=ot, po=po: e.tensor_copy(
                        y_t[ot], PS[:, po:po + 2048]),
                      reads=[P(po, 2048)], writes=[A(YS + ot * 2048, 2048)])
                else:
                    q("vector", lambda e, ot=ot, po=po: e.tensor_tensor(
                        y_t[ot], PS[:, po:po + 2048], y_t[ot], op=ALU.add),
                      reads=[P(po, 2048), A(YS + ot * 2048, 2048)],
                      writes=[A(YS + ot * 2048, 2048)])
                if last:
                    fb = ot % 2
                    q("scalar", lambda e, ot=ot: e.activation(xn_t[ot], y_t[ot], AF.Silu),
                      reads=[A(YS + ot * 2048, 2048)], writes=[XNK(ot)])
                    q("vector", lambda e, ot=ot, fb=fb: e.tensor_tensor(
                        t1f[fb], s_sb[ot], xn_t[ot], op=ALU.mult),
                      reads=[A(SBF + ot * 1024, 1024), XNK(ot)],
                      writes=[A(FIN + 4096 + fb * 2048, 2048)])
                    q("gpsimd", lambda e, ot=ot, fb=fb: e.tensor_tensor(
                        t1f[fb], t1f[fb], xfin[fb], op=ALU.add),
                      reads=[A(FIN + 4096 + fb * 2048, 2048), A(FIN + fb * 2048, 2048)],
                      writes=[A(FIN + 4096 + fb * 2048, 2048)])
                    q("sync", lambda e, ot=ot, fb=fb: e.dma_start(
                        out=out_d[ot * 128:(ot + 1) * 128, :], in_=t1f[fb]),
                      reads=[A(FIN + 4096 + fb * 2048, 2048)],
                      writes=[f"outd{ot}"], dma=True)
                    if ot + 2 < NDT:
                        q("gpsimd", lambda e, ot=ot, fb=fb: e.dma_start(
                            out=xfin[fb], in_=xT[(ot + 2) * 128:(ot + 3) * 128, :]),
                          writes=[A(FIN + fb * 2048, 2048)], dma=True)

        if dbg:
            for k in range(NDT):
                q("gpsimd", lambda e, k=k: e.dma_start(
                    out=dbg_xn[k * 128:(k + 1) * 128, :], in_=xn_t[k]),
                  reads=[A(XN + k * 1024, 1024)], writes=[f"dbgxn{k}"], dma=True)
                q("gpsimd", lambda e, k=k: e.dma_start(
                    out=dbg_s[k * 128:(k + 1) * 128, :], in_=s_sb[k]),
                  reads=[A(SBF + k * 1024, 1024)], writes=[f"dbgs{k}"], dma=True)
                q("gpsimd", lambda e, k=k: e.dma_start(
                    out=dbg_cmb[k * 128:(k + 1) * 128, :], in_=cmb_t[k]),
                  reads=[A(CMB + k * 1024, 1024)], writes=[f"dbgc{k}"], dma=True)
                if phases >= 3:
                    q("sync", lambda e, k=k: e.dma_start(
                        out=dbg_y[k * 128:(k + 1) * 128, :], in_=y_t[k]),
                      reads=[A(YS + k * 2048, 2048)], writes=[f"dbgy{k}"], dma=True)
            q("sync", lambda e: e.dma_start(out=dbg_comb, in_=comb),
              reads=[A(P2, 256)], writes=["dbgcomb"], dma=True)
            q("gpsimd", lambda e: e.nop(), wait_all_of="gpsimd")

        q("sync", lambda e: e.nop(), wait_all_of="sync")

        b.emit(block, sems)

    return nc


_NC_CACHE = {}


def _get_nc():
    if "nc" not in _NC_CACHE:
        _NC_CACHE["nc"] = build_program()
    return _NC_CACHE["nc"]


def _host_constants():
    dloc = np.zeros((NP, 128, 256), np.float64)
    qds = np.zeros((NP, 128, 128), np.float64)
    kdv = np.zeros((128, NH), np.float64)
    idx = np.arange(128)
    for hp in range(NP):
        for h01 in range(2):
            g = GAMMAS[2 * hp + h01]
            diff = idx[None, :] - idx[:, None]
            dl = np.where(diff >= 0, g ** np.maximum(diff, 0) / 8.0, 0.0)
            dloc[hp, :, h01 * 128:(h01 + 1) * 128] = dl
            qds[hp, h01 * 64:(h01 + 1) * 64, :] = (g ** (idx[None, :] + 1.0)) / 8.0
    for h in range(NH):
        kdv[:, h] = GAMMAS[h] ** (127.0 - idx)
    g128 = np.zeros((128, NP), np.float64)
    for hp in range(NP):
        g128[0:64, hp] = GAMMAS[2 * hp] ** 128
        g128[64:128, hp] = GAMMAS[2 * hp + 1] ** 128
    return (dloc.astype(np.float32), qds.astype(np.float32), kdv.astype(np.float32),
            g128.astype(np.float32))


def kernel(L0_attn_x, L0_attn_Wq, L0_attn_Wk, L0_attn_Wv, L0_attn_gn_weight,
           L0_attn_gn_bias, L0_attn_Wout, L0_ffn_router_weight, L0_ffn_router_bias,
           L0_ffn_w_gate, L0_ffn_w_up, L0_ffn_w_down):
    x = np.asarray(L0_attn_x)
    B = x.shape[0]
    wq_cat = np.ascontiguousarray(np.asarray(L0_attn_Wq).transpose(1, 0, 2).reshape(D, D))
    wk_cat = np.asarray(L0_attn_Wk).transpose(1, 0, 2).reshape(D, D)
    wv_cat = np.asarray(L0_attn_Wv).transpose(1, 0, 2).reshape(D, D)
    wvk = np.empty((D, 2 * D), np.float32)
    for hp in range(NP):
        wvk[:, hp * 256:hp * 256 + 128] = wv_cat[:, hp * 128:(hp + 1) * 128]
        wvk[:, hp * 256 + 128:hp * 256 + 256] = wk_cat[:, hp * 128:(hp + 1) * 128]
    dloc, qds, kdv, g128 = _host_constants()
    shared = {
        "wq": wq_cat.astype(np.float32, copy=False),
        "wvk": wvk,
        "wout": np.ascontiguousarray(np.asarray(L0_attn_Wout), dtype=np.float32),
        "wrt": np.ascontiguousarray(np.asarray(L0_ffn_router_weight).T, dtype=np.float32),
        "rbias": np.ascontiguousarray(np.asarray(L0_ffn_router_bias).reshape(1, 16),
                                      dtype=np.float32),
        "gnw": np.ascontiguousarray(np.asarray(L0_attn_gn_weight).reshape(8, 128).T,
                                    dtype=np.float32),
        "gnb": np.ascontiguousarray(np.asarray(L0_attn_gn_bias).reshape(8, 128).T,
                                    dtype=np.float32),
        "wg": np.ascontiguousarray(np.asarray(L0_ffn_w_gate).transpose(0, 2, 1),
                                   dtype=np.float32),
        "wu": np.ascontiguousarray(np.asarray(L0_ffn_w_up).transpose(0, 2, 1),
                                   dtype=np.float32),
        "wd": np.ascontiguousarray(np.asarray(L0_ffn_w_down).transpose(0, 2, 1),
                                   dtype=np.float32),
        "dloc": dloc, "qds": qds, "kdv": kdv, "g128": g128,
        "ident": np.eye(128, dtype=np.float32),
    }
    in_maps = []
    for bi in range(B):
        m = dict(shared)
        m["xT"] = np.ascontiguousarray(x[bi].T)
        in_maps.append(m)
    nc = _get_nc()
    res = run_bass_kernel_spmd(nc, in_maps, core_ids=list(range(B)))
    out = np.stack([np.asarray(r["out"]).T for r in res.results], axis=0)
    return np.ascontiguousarray(out.astype(np.float32))
